# version 08
# baseline (speedup 1.0000x reference)
import numpy as np
import ml_dtypes
import concourse.bass as bass
import concourse.mybir as mybir
from concourse.bass_utils import run_bass_kernel_spmd

F32 = mybir.dt.float32
BF16 = mybir.dt.bfloat16
I32 = mybir.dt.int32
AF = mybir.ActivationFunctionType
ALU = mybir.AluOpType
X = mybir.AxisListType.X
ds = bass.ds

MAGIC = 77001.0
MAGIC_I = int(np.float32(MAGIC).view(np.int32))
POLL_LIMIT = 30000

H, DM, KCH = 16, 1024, 8
NEXP, HIDC = 18, 4
EPS_LN, EPS_N = 1e-5, 1e-4
G = 256

REAL = dict(NO=16)


class Prog:
    ENG = ("sync", "gpsimd", "tensor", "scalar", "vector")

    def __init__(self):
        self.t = {e: [] for e in self.ENG}

    def add(self, eng, fn):
        self.t[eng].append(fn)


class Q:
    def __init__(self, prog, eng, sem, step=1):
        self.p, self.eng, self.sem, self.step, self.n = prog, eng, sem, step, 0

    def do(self, fn, inc=True):
        if inc:
            self.n += self.step
            sem, st = self.sem, self.step
            self.p.add(self.eng, lambda e, fn=fn, sem=sem, st=st: fn(e).then_inc(sem, st))
        else:
            self.p.add(self.eng, lambda e, fn=fn: fn(e))
        return self.n

    def wait(self, q2, val=None):
        v = q2.n if val is None else val
        if v <= 0:
            return
        sem = q2.sem
        self.p.add(self.eng, lambda e, sem=sem, v=v: e.wait_ge(sem, v))


def build(cfg, exchange="poll", detect_races=True):
    NO = cfg["NO"]
    NF = 3 * NO
    S = NO * 128
    FS = NF * 128
    NC = NO + NF
    n_og, n_fg = NO // 2, NF // 2
    n_ag = S // G
    FH = S // 2
    FSUB = min(512, FH)
    nsub = FH // FSUB

    nc = bass.Bass("TRN2", target_bir_lowering=False,
                   detect_race_conditions=detect_races)

    xTo = nc.dram_tensor("xTo", [DM, S], F32, kind="ExternalInput")
    xTf = nc.dram_tensor("xTf", [DM, FS], F32, kind="ExternalInput")
    wqkv = nc.dram_tensor("wqkv", [DM, 3 * DM], F32, kind="ExternalInput")
    wout = nc.dram_tensor("wout", [DM, DM], F32, kind="ExternalInput")
    gwt = nc.dram_tensor("gwt", [128, 128], F32, kind="ExternalInput")
    wsin = nc.dram_tensor("wsin", [DM, DM], F32, kind="ExternalInput")
    wvin = nc.dram_tensor("wvin", [DM, DM], F32, kind="ExternalInput")
    wsout = nc.dram_tensor("wsout", [DM, DM], F32, kind="ExternalInput")
    ewin = nc.dram_tensor("ewin", [16, DM, 512], F32, kind="ExternalInput")
    evin = nc.dram_tensor("evin", [16, DM, 512], F32, kind="ExternalInput")
    ewout = nc.dram_tensor("ewout", [16, 512, DM], F32, kind="ExternalInput")
    bqt = nc.dram_tensor("bqt", [128, 8], F32, kind="ExternalInput")
    bkv = nc.dram_tensor("bkv", [1, 2048], F32, kind="ExternalInput")
    bot = nc.dram_tensor("bot", [128, 8], F32, kind="ExternalInput")
    cosbo = nc.dram_tensor("cosbo", [128, S], F32, kind="ExternalInput")
    sinbo = nc.dram_tensor("sinbo", [128, S], F32, kind="ExternalInput")
    cost = nc.dram_tensor("cost", [128, NC], F32, kind="ExternalInput")
    sint = nc.dram_tensor("sint", [128, NC], F32, kind="ExternalInput")
    identin = nc.dram_tensor("identin", [128, 128], F32, kind="ExternalInput")
    outT = nc.dram_tensor("outT", [DM, S], F32, kind="ExternalOutput")
    sdata = nc.dram_tensor("sdata", [128, 2080], F32, kind="Internal", addr_space="Shared")
    sflag = nc.dram_tensor("sflag", [1, 16], F32, kind="Internal", addr_space="Shared")
    cc_in = nc.dram_tensor("cc_in", [128, 1040], F32, kind="Internal")
    cc_out = nc.dram_tensor("cc_out", [128, 1040], F32, kind="Internal")

    from contextlib import ExitStack
    with ExitStack() as ctx:
        sb = lambda n, s, d: ctx.enter_context(nc.sbuf_tensor(n, s, d))
        AR = sb("AR", [128, KCH, S], F32)
        AX = sb("AX", [128, KCH, S], BF16)
        AQ = sb("AQ", [128, KCH, S], BF16)
        AW = sb("AW", [128, 24576], BF16)
        AC = sb("AC", [128, 4096], BF16)
        AK = sb("AK", [128, 3136], BF16)
        AE = sb("AE", [128, 2080], F32)
        RBF = sb("RBF", [1, 1024], BF16)
        consts = sb("consts", [128, 8], F32)
        bq_t = sb("bq_t", [128, 8], F32)
        bo_t = sb("bo_t", [128, 8], F32)
        gwTb = sb("gwTb", [128, 128], BF16)
        cosTt = sb("cosTt", [128, NC], F32)
        sinTt = sb("sinTt", [128, NC], F32)
        identb = sb("identb", [128, 128], BF16)
        combT = sb("combT", [128, 16 * NO], BF16)
        crow = sb("crow", [1, 512], BF16)
        ones = sb("ones", [128, 128], BF16)
        xbt = sb("xbt", [128, 1024], BF16)
        LNB = sb("LNB", [128, 1024], BF16)
        LNT = sb("LNT", [128, 256], BF16)
        magic = sb("magic", [1, 8], F32)
        flg = sb("flg", [1, 16], F32)
        pb = [ctx.enter_context(nc.psum_tensor(f"pb{i}", [128, 512], F32))
              for i in range(8)]
        sems = {}
        for sname in ("sXL", "sTB", "sWKV", "sCF", "sXF0", "sXF1", "sWQ",
                      "sWO2", "sWV0", "sWV1", "sWB0", "sWB1", "sZF", "sXW",
                      "sFW", "sPL", "sPR", "sH2", "sPE", "sDVE", "sACT", "sSX"):
            sems[sname] = ctx.enter_context(nc.semaphore(sname))

        prog = Prog()
        ql_x = Q(prog, "sync", sems["sXL"], 16)
        ql_tb = Q(prog, "sync", sems["sTB"], 16)
        ql_wkv = Q(prog, "gpsimd", sems["sWKV"], 16)
        ql_cf = Q(prog, "gpsimd", sems["sCF"], 16)
        ql_xf = [Q(prog, "gpsimd", sems["sXF0"], 16),
                 Q(prog, "gpsimd", sems["sXF1"], 16)]
        ql_wq = Q(prog, "gpsimd", sems["sWQ"], 16)
        ql_wo2 = Q(prog, "gpsimd", sems["sWO2"], 16)
        ql_wv = [Q(prog, "gpsimd", sems["sWV0"], 16),
                 Q(prog, "gpsimd", sems["sWV1"], 16)]
        ql_wb = [Q(prog, "gpsimd", sems["sWB0"], 16),
                 Q(prog, "gpsimd", sems["sWB1"], 16)]
        ql_zf = Q(prog, "sync", sems["sZF"], 16)
        ql_xw = Q(prog, "sync", sems["sXW"], 16)
        ql_fw = Q(prog, "sync", sems["sFW"], 16)
        ql_pr = Q(prog, "sync", sems["sPR"], 16)
        h2 = Q(prog, "sync", sems["sH2"], 16)
        pe = Q(prog, "tensor", sems["sPE"])
        dv = Q(prog, "vector", sems["sDVE"])
        ac = Q(prog, "scalar", sems["sACT"])
        sx = Q(prog, "sync", sems["sSX"])

        W1 = AW[:, 0:8192]

        W2 = AW[:, 8192:16384]
        W3 = AW[:, 16384:20480]
        W4 = AW[:, 20480:24576]
        kcs2 = AK[:, 0:2048].rearrange("p (h x) -> p h x", h=16)
        vt2 = AK[:, 2048:3088].rearrange("p (h x) -> p h x", h=16)
        TMPA = AK[:, 0:1024].bitcast(F32)[:, 0:512]
        TMPB = AK[:, 1024:2048].bitcast(F32)[:, 0:512]

        bkv_r = AW[0:1, 20480:22528]
        RSTD_B = None
        CEPS = consts[:, 0:1]
        CNH = consts[:, 1:2]
        CM1 = consts[:, 3:4]

        ql_x.do(lambda e: e.dma_start(
            out=AR[:, :, :], in_=xTo[:, :].rearrange("(c p) t -> p c t", p=128)))
        for t, src in ((bq_t, bqt), (bo_t, bot), (cosTt, cost), (sinTt, sint)):
            ql_tb.do(lambda e, t=t, src=src: e.dma_start(out=t[:, :], in_=src[:, :]))

        ql_wkv.do(lambda e: e.dma_start(
            out=AW[:, 0:16384].rearrange("p (c f) -> p c f", c=8),
            in_=wqkv[:, 1024:3072].rearrange("(c p) f -> p c f", p=128)))
        ql_cf.do(lambda e: e.dma_start(out=W3[:, 0:S], in_=cosbo[:, :]))
        ql_cf.do(lambda e: e.dma_start(out=W3[:, 2048:2048 + S], in_=sinbo[:, :]))
        ql_cf.do(lambda e: e.dma_start(out=gwTb[:, :], in_=gwt[:, :]))
        ql_cf.do(lambda e: e.dma_start(out=bkv_r[:, :], in_=bkv[:, :]))
        ql_cf.do(lambda e: e.dma_start(out=identb[:, :], in_=identin[:, :]))

        dv.do(lambda e: e.memset(ones[:, :], 1.0))
        dv.do(lambda e: e.memset(consts[:, 0:1], EPS_LN))
        dv.do(lambda e: e.memset(consts[:, 1:2], -0.5))
        dv.do(lambda e: e.memset(consts[:, 2:3], EPS_N))
        dv.do(lambda e: e.memset(consts[:, 3:4], -1.0))
        dv.do(lambda e: e.memset(vt2[:, :, 64:65], 1.0))
        dv.do(lambda e: e.memset(magic[0:1, 0:1], MAGIC))
        dv.do(lambda e: e.memset(magic[0:1, 1:2], 0.0))
        m_consts = dv.n

        ql_zf.wait(dv, m_consts)

        def _zeroflag(e):
            slot = e.partition_id() % 2
            e.dma_start(out=sflag[0:1, ds(slot, 1)], in_=magic[0:1, 1:2])\
                .then_inc(sems["sZF"], 16)
        ql_zf.n += 16
        prog.add("sync", _zeroflag)

        WKV = AW[:, 0:16384].rearrange("p (c f) -> p c f", c=8)
        COSB = W3[:, 0:2048]
        SINB = W3[:, 2048:4096]

        ROW_A = AE[0:1, 0:512]
        ROW_B = AE[0:1, 512:1024]
        RB_STD = RBF[0:1, 0:512]
        RB_NMR = RBF[0:1, 512:1024]
        RSTD_B = LNB[:, 0:512]
        NMR_B = LNB[:, 512:1024]

        def emit_ln(src, cast_needed, apply, extra_waits=None, W=256,
                    xa=None, xs=None):
            pe.wait(dv, emit_ln.last_lnb)
            pe.wait(ac, ac.n)
            ac.wait(dv, dv.n)
            if extra_waits:
                extra_waits()
            m_sq = {}
            for k in range(KCH):
                pp = k % 2
                if cast_needed:
                    xb = xa[:, pp * W:(pp + 1) * W]
                    sq = xs[:, pp * W:(pp + 1) * W]
                else:
                    sq = xbt[:, pp * 512:pp * 512 + W]
                if cast_needed:
                    if k >= 2:
                        dv.wait(pe, m_sq[k - 2][1])
                    dv.do(lambda e, xb=xb, k=k: e.tensor_copy(xb, src(k)))
                    m_cast = dv.n
                    ac.wait(dv, m_cast)
                    if k >= 2:
                        ac.wait(pe, m_sq[k - 2][1])
                    ac.do(lambda e, sq=sq, xb=xb: e.activation(sq, xb, AF.Square))
                    msq_a = ac.n
                    pe.wait(dv, m_cast)
                    xin = xb
                else:
                    if k >= 2:
                        ac.wait(pe, m_sq[k - 2][1])
                    ac.do(lambda e, sq=sq, k=k: e.activation(sq, src(k), AF.Square))
                    msq_a = ac.n
                    xin = None
                pe.wait(ac, msq_a)
                st, sp = (k == 0), (k == KCH - 1)
                if xin is not None:
                    pe.do(lambda e, xin=xin, st=st, sp=sp: e.matmul(
                        pb[6][0:1, 0:W], ones[:, 0:1], xin, start=st, stop=sp))
                else:
                    pe.do(lambda e, k=k, st=st, sp=sp: e.matmul(
                        pb[6][0:1, 0:W], ones[:, 0:1], src(k), start=st, stop=sp))
                pe.do(lambda e, sq=sq, st=st, sp=sp: e.matmul(
                    pb[7][0:1, 0:W], ones[:, 0:1], sq, start=st, stop=sp))
                m_sq[k] = (pe.n - 1, pe.n)
            m_stats = pe.n
            dv.wait(pe, m_stats)
            dv.do(lambda e: e.tensor_scalar(ROW_A[:, 0:W], pb[6][0:1, 0:W],
                                            1.0 / DM, None, ALU.mult))
            dv.do(lambda e: e.tensor_scalar(ROW_B[:, 0:W], pb[7][0:1, 0:W],
                                            1.0 / DM, None, ALU.mult))
            m_mq = dv.n
            dv.wait(dv, m_mq)
            dv.do(lambda e: e.tensor_tensor(pb[6][0:1, 0:W], ROW_A[:, 0:W],
                                            ROW_A[:, 0:W], ALU.mult))
            m_msq = dv.n
            dv.wait(dv, m_msq)
            dv.do(lambda e: e.tensor_tensor(ROW_B[:, 0:W], ROW_B[:, 0:W],
                                            pb[6][0:1, 0:W], ALU.subtract))
            m_var = dv.n
            ac.wait(dv, m_var)
            ac.do(lambda e: e.activation(pb[7][0:1, 0:W], ROW_B[:, 0:W],
                                         AF.Ln, bias=CEPS[0:1, :]))
            m_l = ac.n
            ac.wait(ac, m_l)
            ac.do(lambda e: e.activation(RB_STD[0:1, 0:W], pb[7][0:1, 0:W],
                                         AF.Exp, scale=CNH[0:1, :]))
            m_rstd = ac.n
            dv.wait(ac, m_rstd)
            dv.do(lambda e: e.tensor_tensor(RB_NMR[0:1, 0:W], ROW_A[:, 0:W],
                                            RB_STD[0:1, 0:W], ALU.mult))
            m_rows = dv.n
            pe.wait(dv, m_rows)
            pe.do(lambda e: e.matmul(pb[6][:, 0:W], ones[0:1, :],
                                     RB_STD[0:1, 0:W], start=True, stop=True))
            pe.do(lambda e: e.matmul(pb[7][:, 0:W], ones[0:1, :],
                                     RB_NMR[0:1, 0:W], start=True, stop=True))
            m_bcast = pe.n
            dv.wait(pe, m_bcast)
            dv.do(lambda e: e.tensor_copy(LNB[:, 0:W], pb[6][:, 0:W]))
            dv.do(lambda e: e.tensor_copy(LNB[:, 512:512 + W], pb[7][:, 0:W]))
            m_lnb = dv.n
            emit_ln.last_lnb = m_lnb
            apply()
            return dv.n
        emit_ln.last_lnb = 0

        def apply_to_bf16(dst_of, src_of, W=256):
            def _a():
                for k in range(KCH):
                    for j in range(W // 256):
                        o = j * 256
                        dv.wait(dv, dv.n)
                        dv.do(lambda e, k=k, o=o: e.tensor_tensor(
                            LNT[:, 0:256], src_of(k)[:, o:o + 256],
                            RSTD_B[:, o:o + 256], ALU.mult))
                        mm = dv.n
                        dv.wait(dv, mm)
                        dv.do(lambda e, k=k, o=o: e.tensor_tensor(
                            dst_of(k)[:, o:o + 256], LNT[:, 0:256],
                            NMR_B[:, o:o + 256], ALU.subtract))
            return _a

        def apply_residual(src_of, res_of, W=256):
            def _a():
                for k in range(KCH):
                    for j in range(W // 256):
                        o = j * 256
                        dv.wait(dv, dv.n)
                        dv.do(lambda e, k=k, o=o: e.tensor_tensor(
                            LNT[:, 0:256], src_of(k)[:, o:o + 256],
                            RSTD_B[:, o:o + 256], ALU.mult))
                        mm = dv.n
                        dv.wait(dv, mm)
                        dv.do(lambda e, k=k, o=o: e.tensor_tensor(
                            LNT[:, 0:256], LNT[:, 0:256],
                            NMR_B[:, o:o + 256], ALU.subtract))
                        mm2 = dv.n
                        dv.wait(dv, mm2)
                        dv.do(lambda e, k=k, o=o: e.tensor_tensor(
                            res_of(k)[:, o:o + 256], res_of(k)[:, o:o + 256],
                            LNT[:, 0:256], ALU.add))
            return _a

        XFP = [AQ[:, :, b * 512:(b + 1) * 512] for b in range(min(4, S // 512))]
        NBUF = len(XFP)
        assert NBUF >= 2, "need at least 1024 own tokens per core"
        n_og4 = NO // 4
        n_fg4 = (3 * NO) // 4
        NKVG = n_og4 + n_fg4

        xf_marks = {}
        for fgi in range(min(2, n_fg4)):
            ql_xf[fgi % 2].do(lambda e, fgi=fgi: e.dma_start(
                out=XFP[fgi % NBUF][:, :, :],
                in_=xTf[:, fgi * 512:(fgi + 1) * 512]
                .rearrange("(c p) t -> p c t", p=128)))
            xf_marks[fgi] = (ql_xf[fgi % 2], ql_xf[fgi % 2].n)

        kv_bank_marks = {}
        pe.wait(ql_x)
        pe.wait(ql_tb)
        pe.wait(dv, m_consts)
        pe.wait(ql_wkv)
        pe.wait(ql_cf)
        ac.wait(dv, m_consts)
        ac.wait(ql_tb)
        dv.wait(ql_tb)
        dv.wait(ql_x)

        kvproj_done = {}

        def emit_kvacc(c, m_ev_d, m_ev_a):
            pe.wait(dv, m_ev_d)
            pe.wait(ac, m_ev_a)
            for h in range(H):
                bsel = 0 if h < 7 else (1 if h < 14 else 2)
                accb = pb[2 + bsel]
                col = 65 * (h - (0, 7, 14)[bsel])
                fc = (c == 0) and (h in (0, 7, 14))
                lc = (c == NC - 1) and (h in (6, 13, 15))
                pe.do(lambda e, h=h, accb=accb, col=col, fc=fc, lc=lc:
                      e.matmul(accb[:, col:col + 65], kcs2[:, h, :],
                               vt2[:, h, 0:65],
                               start=fc, stop=lc), inc=(h == H - 1))
            emit_kvacc.last = pe.n
        emit_kvacc.last = 0
        emit_kvacc.pending = []

        def ln_for_group(g):
            own = g < n_og4
            fg = g - n_og4
            if own:
                src = lambda k, g=g: AR[:, k, g * 512:(g + 1) * 512]
                extra = None
                apply = apply_to_bf16(
                    lambda k, g=g: AX[:, k, g * 512:(g + 1) * 512],
                    lambda k, g=g: AR[:, k, g * 512:(g + 1) * 512], W=512)
                xa, xs = AC[:, 0:1024], AC[:, 1024:2048]
            else:
                xfv = XFP[fg % NBUF]
                src = lambda k, xfv=xfv: xfv[:, k, 0:512]
                lq, lv = xf_marks[fg]
                extra = lambda lq=lq, lv=lv: (pe.wait(lq, lv), ac.wait(lq, lv),
                                              dv.wait(lq, lv))
                apply = apply_to_bf16(
                    lambda k, xfv=xfv: xfv[:, k, 0:512],
                    lambda k, xfv=xfv: xfv[:, k, 0:512], W=512)
                xa = xs = None
            return emit_ln(src, cast_needed=own, apply=apply, extra_waits=extra,
                           W=512, xa=xa, xs=xs)

        apply_marks = {0: ln_for_group(0)}
        for g in range(NKVG):
            own = g < n_og4
            fg = g - n_og4
            m_apply = apply_marks[g]

            if own:
                xn_of = lambda k, ci, g=g: AX[:, k, (g * 4 + ci) * 128:
                                              (g * 4 + ci) * 128 + 128]
            else:
                xn_of = lambda k, ci, xfv=XFP[fg % NBUF]: \
                    xfv[:, k, ci * 128:ci * 128 + 128]

            for ci in range(4):
                c = g * 4 + ci
                for n in range(4):
                    bi3 = (ci * 4 + n) % 3
                    bank = pb[(0, 1, 5)[bi3]]
                    prev = kv_bank_marks.get(bi3)
                    if prev is not None:
                        pe.wait(dv, prev[0])
                        pe.wait(ac, prev[1])
                    pe.wait(dv, m_apply)
                    for k in range(KCH):
                        pe.do(lambda e, k=k, ci=ci, n=n, bank=bank, xn_of=xn_of:
                              e.matmul(bank[:, 0:512], xn_of(k, ci),
                                       WKV[:, k, n * 512:(n + 1) * 512],
                                       start=(k == 0), stop=False), inc=False)
                    pe.do(lambda e, n=n, bank=bank: e.matmul(
                        bank[:, 0:512], ones[0:1, :],
                        bkv_r[0:1, n * 512:(n + 1) * 512],
                        start=False, stop=True))
                    m_proj = pe.n
                    if n < 2:
                        dv.wait(pe, m_proj)
                        dv.wait(pe, emit_kvacc.last)
                        bview = bank[:, 0:512].rearrange("p (h x) -> p h x", h=8)
                        dv.do(lambda e, bview=bview, n=n, c=c: e.tensor_scalar(
                            kcs2[:, 8 * n:8 * n + 8, 0:64], bview, 0.0,
                            cosTt[:, c:c + 1], ALU.max, ALU.mult))
                        dv.do(lambda e, bview=bview, n=n, c=c: e.tensor_scalar(
                            kcs2[:, 8 * n:8 * n + 8, 64:128], bview, 0.0,
                            sinTt[:, c:c + 1], ALU.max, ALU.mult))
                        kv_bank_marks[bi3] = (dv.n, ac.n)
                    else:
                        ac.wait(pe, m_proj)
                        ac.wait(pe, emit_kvacc.last)
                        bv2 = bank[:, 0:512].rearrange("p (h x) -> p h x", h=8)
                        ac.do(lambda e, bv2=bv2, n=n: e.copy(
                            vt2[:, 8 * (n - 2):8 * (n - 2) + 8, 0:64], bv2))
                        kv_bank_marks[bi3] = (dv.n, ac.n)
                emit_kvacc(c, dv.n, ac.n)
            kvproj_done[g] = pe.n

            if g + 1 < NKVG:
                apply_marks[g + 1] = ln_for_group(g + 1)

            tgt = (g + 2) - n_og4
            if 2 <= tgt < n_fg4 and tgt not in xf_marks:
                qq = ql_xf[tgt % 2]
                qq.wait(pe, kvproj_done[g])
                qq.do(lambda e, tgt=tgt, b=tgt % NBUF: e.dma_start(
                    out=XFP[b][:, :, :],
                    in_=xTf[:, tgt * 512:(tgt + 1) * 512]
                    .rearrange("(c p) t -> p c t", p=128)))
                xf_marks[tgt] = (qq, qq.n)
        m_kv_mms = pe.n

        kvloc = AE[:, 0:1040]
        partner = AE[:, 1040:2080]
        dv.wait(pe, m_kv_mms)
        dv.do(lambda e: e.tensor_copy(kvloc[:, 0:455], pb[2][:, 0:455]))
        dv.do(lambda e: e.tensor_copy(kvloc[:, 455:910], pb[3][:, 0:455]))
        dv.do(lambda e: e.tensor_copy(kvloc[:, 910:1040], pb[4][:, 0:130]))
        m_kvloc = dv.n

        if exchange == "poll":
            ql_xw.wait(ql_zf)
            ql_xw.wait(dv, m_kvloc)

            def _exchange(e):
                off = (e.partition_id() % 2) * 1040
                e.dma_start(out=sdata[:, ds(off, 1040)], in_=kvloc)\
                    .then_inc(sems["sXW"], 16)
            ql_xw.n += 16
            prog.add("sync", _exchange)
            ql_fw.wait(ql_xw)

            def _flagwrite(e):
                slot = e.partition_id() % 2
                e.dma_start(out=sflag[0:1, ds(slot, 1)], in_=magic[0:1, 0:1])\
                    .then_inc(sems["sFW"], 16)
            ql_fw.n += 16
            prog.add("sync", _flagwrite)

            def _poll(e):
                e.wait_ge(sems["sFW"], 16)
                with e.register("more") as more, e.register("cnt") as cnt, \
                        e.register("it") as it, e.register("exp") as exp:
                    e.reg_mov(more, 1)
                    e.reg_mov(it, 0)
                    e.reg_mov(exp, 0)
                    with e.While(more):
                        e.reg_add(it, it, 1)
                        e.dma_start(out=flg[:, :], in_=sflag[:, :])\
                            .then_inc(sems["sPL"], 16)
                        e.reg_add(exp, exp, 16)
                        e.wait_ge(sems["sPL"], exp)
                        e.reg_mov(cnt, 0)
                        for i in range(2):
                            fv = e.value_load(flg[0:1, i:i + 1].bitcast(I32))
                            with e.register(f"eq{i}") as req:
                                e.reg_alu(req, fv, MAGIC_I, ALU.is_equal)
                                e.reg_alu(cnt, cnt, req, ALU.add)
                        e.reg_alu(more, cnt, 2, ALU.is_lt)
                        with e.register("lim") as lim:
                            e.reg_alu(lim, it, POLL_LIMIT, ALU.is_lt)
                            e.reg_alu(more, more, lim, ALU.bitwise_and)
                    poff = (1 - (e.partition_id() % 2)) * 1040
                    e.dma_start(out=partner, in_=sdata[:, ds(poff, 1040)])\
                        .then_inc(sems["sPR"], 16)
            prog.add("sync", _poll)
            ql_pr.n += 16
            sx.wait(ql_pr)
            prog.add("sync", lambda e: e.sem_inc(sems["sSX"], 1))
            sx.n += 1
            m_xdone = sx.n
        else:
            ql_xw.wait(dv, m_kvloc)
            ql_xw.do(lambda e: e.dma_start(out=cc_in[:, :], in_=kvloc))
            prog.add("gpsimd", lambda e: e.wait_ge(sems["sXW"], ql_xw.n))
            prog.add("gpsimd", lambda e: e.collective_compute(
                "AllReduce", ALU.add,
                replica_groups=[[0, 1], [2, 3], [4, 5], [6, 7]],
                ins=[cc_in[:, :].opt()],
                outs=[cc_out[:, :].opt()]).then_inc(sems["sPR"], 16))
            ql_pr.n += 16
            prog.add("sync", lambda e, v=ql_pr.n: e.wait_ge(sems["sPR"], v))
            ql_fw.do(lambda e: e.dma_start(out=kvloc, in_=cc_out[:, :]))
            dv.do(lambda e: e.memset(partner, 0.0))
            sx.wait(ql_fw)
            prog.add("sync", lambda e: e.sem_inc(sems["sSX"], 1))
            sx.n += 1
            m_xdone = sx.n

        dv.wait(sx, m_xdone)
        dv.wait(dv, dv.n)
        dv.do(lambda e: e.tensor_tensor(kvloc, kvloc, partner, ALU.add))
        m_sum = dv.n
        dv.wait(dv, m_sum)
        stats_bf = partner.bitcast(BF16)
        SB_COS = stats_bf[0:64, 0:1040]
        SB_SIN = stats_bf[0:64, 1040:2080]
        dv.do(lambda e: e.tensor_copy(SB_COS, kvloc[0:64, :]))
        dv.do(lambda e: e.tensor_copy(SB_SIN, kvloc[64:128, :]))
        m_stats_bf = dv.n

        ql_wq.wait(pe, m_kv_mms)
        ql_wq.do(lambda e: e.dma_start(
            out=W1[:, :].rearrange("p (c f) -> p c f", c=8),
            in_=wqkv[:, 0:1024].rearrange("(c p) f -> p c f", p=128)))
        ql_wo2.wait(pe, m_kv_mms)
        ql_wo2.do(lambda e: e.dma_start(
            out=W2[:, :].rearrange("p (c f) -> p c f", c=8),
            in_=wout[:, :].rearrange("(c p) f -> p c f", p=128)))
        WQ = W1[:, :].rearrange("p (c f) -> p c f", c=8)
        WO = W2[:, :].rearrange("p (c f) -> p c f", c=8)

        QR = [W4[0:64, 0:256], W4[0:64, 256:512]]
        QC = [W4[0:64, 512:768], W4[0:64, 768:1024]]
        QS = [W4[0:64, 1024:1280], W4[0:64, 1280:1536]]
        CS = [W4[:, 1536:2048].bitcast(F32)[0:64, 0:256],
              W4[:, 2048:2560].bitcast(F32)[0:64, 0:256]]
        CSROW = [W4[:, 1536:2048].bitcast(F32)[0:1, 0:256],
                 W4[:, 2048:2560].bitcast(F32)[0:1, 0:256]]
        RNR = [W4[0:1, 2560:2816], W4[0:1, 2816:3072]]

        pe.wait(ql_wq)
        pe.wait(dv, m_stats_bf)
        ac.wait(dv, m_stats_bf)
        NIT = n_ag * H
        M = [dict() for _ in range(NIT)]

        def hg(it):
            return it // H, it % H

        def emit_q(it):
            g, h = hg(it)
            pp = it % 2
            qb = pb[4 + pp]
            bq_col = bq_t[64 * (h % 2):64 * (h % 2) + 64, h // 2:h // 2 + 1]
            if it >= 2:
                pe.wait(ac, M[it - 2]["qr"])
            for k in range(KCH):
                pe.do(lambda e, k=k, g=g, h=h, qb=qb: e.matmul(
                    qb[0:64, 0:256], WQ[:, k, 64 * h:64 * h + 64],
                    AX[:, k, g * 256:(g + 1) * 256],
                    start=(k == 0), stop=(k == KCH - 1)),
                    inc=(k == KCH - 1))
            M[it]["q"] = pe.n
            ac.wait(pe, M[it]["q"])
            if it >= 2:
                ac.wait(dv, M[it - 2]["qcs"])
            ac.do(lambda e, qb=qb, pp=pp, bq_col=bq_col: e.activation(
                QR[pp], qb[0:64, 0:256], AF.Relu, bias=bq_col))
            M[it]["qr"] = ac.n
            dv.wait(ac, M[it]["qr"])
            if it >= 2:
                dv.wait(pe, M[it - 2]["ctx"])
            dv.do(lambda e, pp=pp, g=g: e.tensor_tensor(
                QC[pp], QR[pp], COSB[0:64, g * 256:(g + 1) * 256], ALU.mult))
            dv.do(lambda e, pp=pp, g=g: e.tensor_tensor(
                QS[pp], QR[pp], SINB[0:64, g * 256:(g + 1) * 256], ALU.mult))
            M[it]["qcs"] = dv.n

        def emit_ctx(it):
            g, h = hg(it)
            pp = it % 2
            cb = pb[pp]
            pe.wait(dv, M[it]["qcs"])
            if it >= 2:
                pe.wait(ac, M[it - 2]["cs"])
            pe.do(lambda e, h=h, pp=pp, cb=cb: e.matmul(
                cb[0:65, 0:256], SB_COS[:, 65 * h:65 * h + 65], QC[pp],
                start=True, stop=False), inc=False)
            pe.do(lambda e, h=h, pp=pp, cb=cb: e.matmul(
                cb[0:65, 0:256], SB_SIN[:, 65 * h:65 * h + 65], QS[pp],
                start=False, stop=True))
            M[it]["ctx"] = pe.n
            ac.wait(pe, M[it]["ctx"])
            if it >= 2:
                ac.wait(dv, M[it - 2]["tt"])
            ac.do(lambda e, cb=cb, pp=pp: e.activation(
                CSROW[pp], cb[64:65, 0:256], AF.Ln, bias=consts[64:65, 2:3]))
            m_l = ac.n
            ac.wait(ac, m_l)
            if it >= 2:
                ac.wait(pe, M[it - 2]["nb"])
            ac.do(lambda e, pp=pp: e.activation(
                RNR[pp][0:1, 0:256], CSROW[pp], AF.Exp,
                scale=consts[0:1, 3:4]))
            M[it]["rn"] = ac.n
            ac.wait(ac, M[it]["rn"])
            ac.do(lambda e, cb=cb, pp=pp: e.copy(CS[pp], cb[0:64, 0:256]))
            M[it]["cs"] = ac.n

        def emit_nb(it):
            g, h = hg(it)
            pp = it % 2
            nb = pb[2 + pp]
            pe.wait(ac, M[it]["rn"])
            if it >= 2:
                pe.wait(dv, M[it - 2]["tt"])
            pe.do(lambda e, pp=pp, nb=nb: e.matmul(
                nb[0:64, 0:256], ones[0:1, 0:64], RNR[pp][0:1, 0:256],
                start=True, stop=True))
            M[it]["nb"] = pe.n
            dv.wait(pe, M[it]["nb"])
            dv.wait(ac, M[it]["cs"])
            ro = 64 * (h % 2)
            dv.do(lambda e, h=h, g=g, pp=pp, nb=nb, ro=ro: e.tensor_tensor(
                AQ[ro:ro + 64, h // 2, g * 256:(g + 1) * 256],
                CS[pp], nb[0:64, 0:256], ALU.mult))
            M[it]["tt"] = dv.n

        for it in range(NIT + 2):
            if it < NIT:
                emit_q(it)
            if 0 <= it - 1 < NIT:
                emit_ctx(it - 1)
            if 0 <= it - 2 < NIT:
                emit_nb(it - 2)
        m_ctx_done = dv.n

        YB2 = W1[:, 0:4096].rearrange("p (c t) -> p c t", c=8)
        pe.wait(ql_wo2)
        pe.wait(dv, m_ctx_done)
        op_evac = {}

        def emit_outproj(g):
            yo = (g % 2) * 256
            for m in range(KCH):
                bank = pb[m % 2]
                prev = op_evac.get((g, m - 2)) or op_evac.get((g - 1, m + 6))
                if prev is not None:
                    pe.wait(dv, prev)
                for k in range(KCH):
                    pe.do(lambda e, k=k, m=m, g=g, bank=bank: e.matmul(
                        bank[:, 0:256], WO[:, k, 128 * m:128 * m + 128],
                        AQ[:, k, g * 256:(g + 1) * 256],
                        start=(k == 0), stop=(k == KCH - 1)),
                        inc=(k == KCH - 1))
                m_mm = pe.n
                dv.wait(pe, m_mm)
                dv.do(lambda e, m=m, bank=bank, yo=yo: e.tensor_scalar(
                    YB2[:, m, yo:yo + 256], bank[:, 0:256], bo_t[:, m:m + 1],
                    None, ALU.add))
                op_evac[(g, m)] = dv.n
            return dv.n

        def emit_ln3(g):
            yo = (g % 2) * 256
            m_y = op_evac[(g, KCH - 1)]
            emit_ln(lambda k, yo=yo: YB2[:, k, yo:yo + 256], cast_needed=False,
                    apply=apply_residual(
                        lambda k, yo=yo: YB2[:, k, yo:yo + 256],
                        lambda k, g=g: AR[:, k, g * 256:(g + 1) * 256]),
                    extra_waits=lambda m_y=m_y: pe.wait(dv, m_y), W=256)

        for g in range(n_ag + 1):
            if g < n_ag:
                emit_outproj(g)
            if g >= 1:
                emit_ln3(g - 1)
        m_x1 = dv.n

        gT = AK[:, 0:64].bitcast(F32)[:, 0:16]
        top8 = AK[:, 64:128].bitcast(F32)[:, 0:8]
        msk = AK[:, 128:192].bitcast(F32)[:, 0:16]
        gsum = AK[:, 192:200].bitcast(F32)[:, 0:1]
        combine_F = partner.bitcast(BF16)[0:16, 0:2048]
        gate_state = {"m_sig": 0}

        ln2_apply = {}

        def emit_ln2(g):
            emit_ln(lambda k, g=g: AR[:, k, g * 256:(g + 1) * 256],
                    cast_needed=True,
                    apply=apply_to_bf16(
                        lambda k, g=g: AX[:, k, g * 256:(g + 1) * 256],
                        lambda k, g=g: AR[:, k, g * 256:(g + 1) * 256]),
                    W=256, xa=xbt[:, 0:512], xs=xbt[:, 512:1024])
            ln2_apply[g] = dv.n
            return dv.n

        def emit_gate(c):
            if gate_state["m_sig"]:
                pe.wait(ac, gate_state["m_sig"])
            pe.wait(dv, ln2_apply[c // 2])
            pe.wait(dv, emit_ln.last_lnb)
            for k in range(KCH):
                pe.do(lambda e, k=k, c=c: e.matmul(
                    pb[6][0:128, 0:16], AX[:, k, c * 128:(c + 1) * 128],
                    gwTb[:, 16 * k:16 * k + 16],
                    start=(k == 0), stop=(k == KCH - 1)), inc=(k == KCH - 1))
            m_g = pe.n
            ac.wait(pe, m_g)
            ac.wait(dv, dv.n)
            ac.do(lambda e: e.activation(gT, pb[6][:, 0:16], AF.Sigmoid))
            m_sig = ac.n
            gate_state["m_sig"] = m_sig
            dv.wait(ac, m_sig)
            dv.do(lambda e: e.max(out=top8, in_=gT))
            dv.wait(dv, dv.n)
            dv.do(lambda e: e.tensor_scalar(msk, gT, top8[:, 3:4], None, ALU.is_ge))
            dv.wait(dv, dv.n)
            dv.do(lambda e: e.tensor_tensor(msk, msk, gT, ALU.mult))
            dv.wait(dv, dv.n)
            dv.do(lambda e: e.reduce_sum(gsum, msk, axis=X))
            dv.wait(dv, dv.n)
            dv.do(lambda e: e.reciprocal(gsum, gsum))
            dv.wait(dv, dv.n)
            dv.do(lambda e, c=c: e.tensor_scalar(
                combT[:, c * 16:(c + 1) * 16], msk, gsum[:, 0:1], None, ALU.mult))

        def emit_ln2gate_fixed(half):
            g0, g1 = half * (n_ag // 2), (half + 1) * (n_ag // 2)
            prev = None
            for g in range(g0, g1):
                emit_ln2(g)
                if prev is not None:
                    for cc2 in (2 * prev, 2 * prev + 1):
                        emit_gate(cc2)
                prev = g
            for cc2 in (2 * prev, 2 * prev + 1):
                emit_gate(cc2)
            return dv.n

        m_c_half = {}
        m_c_half[0] = emit_ln2gate_fixed(0)
        pieces = []
        g0h1, g1h1 = n_ag // 2, n_ag
        for g in range(g0h1, g1h1):
            pieces.append(lambda g=g: emit_ln2(g))
            pieces.append(lambda g=g: emit_gate(2 * g))
            pieces.append(lambda g=g: emit_gate(2 * g + 1))
        m_xn2 = dv.n

        ACCF = AQ[:, :, :].bitcast(F32)
        HEV = AC[:, 0:4 * FH].rearrange("p (c t) -> p c t", c=4)
        WVA = [W1[:, :].rearrange("p (c f) -> p c f", c=16),
               W2[:, :].rearrange("p (c f) -> p c f", c=16)]
        WOB = [W3[:, 0:4096].rearrange("p (c f) -> p c f", c=4),
               W4[:, 0:4096].rearrange("p (c f) -> p c f", c=4)]
        SIL = AK[:, 0:1024].bitcast(F32)[:, 0:512]
        TMP = AK[:, 1024:2048].bitcast(F32)[:, 0:512]

        def win_ap(e_):
            if e_ < 16:
                return ewin[e_].rearrange("(c p) h -> p c h", p=128)
            return wsin[:, (e_ - 16) * 512:(e_ - 15) * 512]\
                .rearrange("(c p) h -> p c h", p=128)

        def vin_ap(e_):
            if e_ < 16:
                return evin[e_].rearrange("(c p) h -> p c h", p=128)
            return wvin[:, (e_ - 16) * 512:(e_ - 15) * 512]\
                .rearrange("(c p) h -> p c h", p=128)

        def wout_ap(e_):
            if e_ < 16:
                return ewout[e_].rearrange("(c p) d -> p c d", p=128)
            return wsout[(e_ - 16) * 512:(e_ - 15) * 512, :]\
                .rearrange("(c p) d -> p c d", p=128)

        wv_marks, wo_marks, a_done, b_done = {}, {}, {}, {}

        def load_wv(idx, e_):
            bi = idx % 2
            if (idx - 2) in a_done:
                ql_wv[bi].wait(pe, a_done[idx - 2])
            ql_wv[bi].do(lambda e, e_=e_, bi=bi: e.dma_start(
                out=WVA[bi][:, 0:8, :], in_=win_ap(e_)))
            ql_wv[bi].do(lambda e, e_=e_, bi=bi: e.dma_start(
                out=WVA[bi][:, 8:16, :], in_=vin_ap(e_)))
            wv_marks[idx] = (ql_wv[bi], ql_wv[bi].n)

        def load_wo(idx, e_):
            bi = idx % 2
            if (idx - 2) in b_done:
                ql_wb[bi].wait(pe, b_done[idx - 2])
            ql_wb[bi].do(lambda e, e_=e_, bi=bi: e.dma_start(
                out=WOB[bi][:, :, :], in_=wout_ap(e_)))
            wo_marks[idx] = (ql_wb[bi], ql_wb[bi].n)

        for qq in (ql_wv[0], ql_wv[1], ql_wb[0], ql_wb[1]):
            qq.wait(dv, m_x1)
            qq.wait(dv, m_ctx_done)
            qq.wait(ac, ac.n)

        def emit_ln4_store(half):
            m_acc = dv.n
            WL = 256 if half == 0 else 512
            xa = xbt[:, 0:512] if WL == 256 else AC[:, 0:1024]
            xs = xbt[:, 512:1024] if WL == 256 else AC[:, 1024:2048]
            for gg in range(FH // WL):
                goff = half * FH + gg * WL
                loff = gg * WL
                emit_ln(lambda k, loff=loff, WL=WL: ACCF[:, k, loff:loff + WL],
                        cast_needed=True,
                        apply=apply_residual(
                            lambda k, loff=loff, WL=WL: ACCF[:, k, loff:loff + WL],
                            lambda k, goff=goff, WL=WL: AR[:, k, goff:goff + WL],
                            W=WL),
                        extra_waits=lambda m_acc=m_acc: pe.wait(dv, m_acc),
                        W=WL, xa=xa, xs=xs)
                m_g_done = dv.n
                h2.wait(dv, m_g_done)
                h2.do(lambda e, half=half, goff=goff, WL=WL: e.dma_start(
                    out=outT[:, goff:goff + WL]
                    .rearrange("(c p) t -> p c t", p=128),
                    in_=AR[:, :, goff:goff + WL]))

        seq = [(half, e_) for half in range(2) for e_ in range(NEXP)]
        load_wv(0, seq[0][1])
        load_wo(0, seq[0][1])
        load_wv(1, seq[1][1])
        load_wo(1, seq[1][1])

        for idx, (half, e_) in enumerate(seq):
            bi = idx % 2
            gated = e_ < 16
            hbase = half * FH
            pe.wait(*wv_marks[idx])
            if idx == 0:
                pe.wait(dv, m_c_half[0])
            if idx == NEXP:
                pe.wait(dv, m_c_half[1])
            chain_marks = {}
            for sub in range(nsub):
                t0 = hbase + sub * FSUB
                ncw = FSUB // 128
                if gated:
                    cbb = pb[4 + sub % 2]
                    pe.wait(dv, dv.n)
                    for ci in range(ncw):
                        cgl = (t0 // 128) + ci
                        pe.do(lambda e, e_=e_, cgl=cgl, ci=ci, cbb=cbb: e.matmul(
                            cbb[0:1, ci * 128:ci * 128 + 128],
                            combT[:, cgl * 16 + e_:cgl * 16 + e_ + 1],
                            identb[:, :], start=True, stop=True))
                    m_sel = pe.n
                    csl = crow[0:1, 0:FSUB]
                    dv.wait(pe, m_sel)
                    dv.do(lambda e, csl=csl, cbb=cbb, w=FSUB: e.tensor_copy(
                        csl, cbb[0:1, 0:w]))
                    m_crow = dv.n
                    pe.wait(dv, m_crow)
                    pe.do(lambda e, csl=csl, cbb=cbb, w=FSUB: e.matmul(
                        cbb[:, 0:w], ones[0:1, :], csl, start=True, stop=True))
                for hc in range(HIDC):
                    hb = pb[hc % 2]
                    vb = pb[2 + hc % 2]
                    prevc = chain_marks.get((sub, hc - 2))
                    if prevc is not None:
                        pe.wait(dv, prevc[0])
                        pe.wait(ac, prevc[1])
                    for k in range(KCH):
                        pe.do(lambda e, k=k, hc=hc, t0=t0, bi=bi, hb=hb, w=FSUB:
                              e.matmul(hb[:, 0:w],
                                       WVA[bi][:, k, 128 * hc:128 * hc + 128],
                                       AX[:, k, t0:t0 + w],
                                       start=(k == 0), stop=(k == KCH - 1)),
                              inc=(k == KCH - 1))
                    m_hin = pe.n
                    for k in range(KCH):
                        pe.do(lambda e, k=k, hc=hc, t0=t0, bi=bi, vb=vb, w=FSUB:
                              e.matmul(vb[:, 0:w],
                                       WVA[bi][:, 8 + k, 128 * hc:128 * hc + 128],
                                       AX[:, k, t0:t0 + w],
                                       start=(k == 0), stop=(k == KCH - 1)),
                              inc=(k == KCH - 1))
                    m_vin = pe.n
                    ac.wait(pe, m_hin)
                    prevt = chain_marks.get((sub, hc - 1))
                    if prevt is not None:
                        ac.wait(dv, prevt[0])
                    ac.do(lambda e, hb=hb, w=FSUB: e.activation(
                        SIL[:, 0:w], hb[:, 0:w], AF.Silu))
                    m_sil = ac.n
                    dv.wait(ac, m_sil)
                    dv.wait(pe, m_vin)
                    ht = HEV[:, hc, sub * FSUB:sub * FSUB + FSUB]
                    if gated:
                        dv.do(lambda e, vb=vb, w=FSUB: e.tensor_tensor(
                            TMP[:, 0:w], SIL[:, 0:w], vb[:, 0:w], ALU.mult))
                        mt = dv.n
                        dv.wait(dv, mt)
                        cbb = pb[4 + sub % 2]
                        dv.do(lambda e, ht=ht, cbb=cbb, w=FSUB: e.tensor_tensor(
                            ht, TMP[:, 0:w], cbb[:, 0:w], ALU.mult))
                    else:
                        dv.do(lambda e, ht=ht, vb=vb, w=FSUB: e.tensor_tensor(
                            ht, SIL[:, 0:w], vb[:, 0:w], ALU.mult))
                    chain_marks[(sub, hc)] = (dv.n, ac.n)
            a_done[idx] = pe.n
            m_h = dv.n
            if half == 0 and pieces:
                pieces.pop(0)()
                if e_ == 0 and pieces:
                    pieces.pop(0)()
                if not pieces:
                    m_c_half[1] = dv.n
            if half == 1 and e_ == 0:
                emit_ln4_store(0)
            pe.wait(*wo_marks[idx])
            pe.wait(dv, m_h)
            badd = {}
            if e_ == 0:
                pe.wait(ac, ac.n)
            for sub in range(nsub):
                for m in range(KCH):
                    mi = m + sub * KCH
                    bb = pb[4 + mi % 2]
                    prevb = badd.get(mi - 2)
                    if prevb is not None:
                        pe.wait(dv, prevb)
                    for k in range(HIDC):
                        pe.do(lambda e, k=k, m=m, sub=sub, bi=bi, bb=bb, w=FSUB:
                              e.matmul(bb[:, 0:w],
                                       WOB[bi][:, k, 128 * m:128 * m + 128],
                                       HEV[:, k, sub * FSUB:sub * FSUB + w],
                                       start=(k == 0), stop=(k == HIDC - 1)),
                              inc=(k == HIDC - 1))
                    m_b = pe.n
                    dv.wait(pe, m_b)
                    accsl = ACCF[:, m, sub * FSUB:(sub + 1) * FSUB]
                    if e_ == 0:
                        dv.do(lambda e, bb=bb, accsl=accsl, w=FSUB:
                              e.tensor_copy(accsl, bb[:, 0:w]))
                    else:
                        dv.do(lambda e, bb=bb, accsl=accsl, w=FSUB:
                              e.tensor_tensor(accsl, accsl, bb[:, 0:w], ALU.add))
                    badd[mi] = dv.n
            b_done[idx] = pe.n
            if idx + 2 < len(seq):
                load_wv(idx + 2, seq[idx + 2][1])
                load_wo(idx + 2, seq[idx + 2][1])

        emit_ln4_store(1)
        m_store = h2.n
        prog.add("sync", lambda e: e.wait_ge(sems["sH2"], m_store))

        with nc.Block() as blk:
            @blk.sync
            def _(e):
                for fn in prog.t["sync"]:
                    fn(e)

            @blk.gpsimd
            def _(e):
                for fn in prog.t["gpsimd"]:
                    fn(e)

            @blk.tensor
            def _(e):
                for fn in prog.t["tensor"]:
                    fn(e)

            @blk.scalar
            def _(e):
                for fn in prog.t["scalar"]:
                    fn(e)

            @blk.vector
            def _(e):
                for fn in prog.t["vector"]:
                    fn(e)

    return nc


def make_in_maps(cfg, A):
    NO = cfg["NO"]
    NF = 3 * NO
    S, FS = NO * 128, NF * 128
    L = 8 * S
    x = np.ascontiguousarray(A["x"].reshape(L, DM).astype(np.float32))
    xT = np.ascontiguousarray(x.T)
    pos = np.arange(L, dtype=np.float64)
    cw = np.cos(np.pi / 2 * pos / L).astype(np.float32)
    sw = np.sin(np.pi / 2 * pos / L).astype(np.float32)

    b_qkv = A["b_qkv"].astype(np.float32)
    com = dict(
        wqkv=np.ascontiguousarray(A["W_qkv"].astype(np.float32)),
        wout=np.ascontiguousarray(A["W_out"].astype(np.float32)),
        gwt=np.ascontiguousarray(
            A["gate_w"].astype(np.float32).T.reshape(8, 128, 16)
            .transpose(1, 0, 2).reshape(128, 128)),
        wsin=np.ascontiguousarray(A["ws_in"].astype(np.float32)),
        wvin=np.ascontiguousarray(A["vs_in"].astype(np.float32)),
        wsout=np.ascontiguousarray(A["ws_out"].astype(np.float32)),
        ewin=np.ascontiguousarray(A["e_win"].astype(np.float32)),
        evin=np.ascontiguousarray(A["e_vin"].astype(np.float32)),
        ewout=np.ascontiguousarray(A["e_wout"].astype(np.float32)),
        bqt=np.ascontiguousarray(b_qkv[0:1024].reshape(8, 128).T),
        bkv=np.ascontiguousarray(b_qkv[1024:3072].reshape(1, 2048)),
        bot=np.ascontiguousarray(A["b_out"].astype(np.float32).reshape(8, 128).T),
        identin=np.eye(128, dtype=np.float32),
    )
    maps = []
    for c in range(8):
        own = np.arange(c * S, (c + 1) * S)
        pair = c // 2
        fidx = np.concatenate([np.arange(0, pair * 2 * S),
                               np.arange((pair + 1) * 2 * S, L)])
        fidx = fidx[:FS] if c % 2 == 0 else fidx[FS:]
        chunk_pos = np.concatenate([own, fidx])
        m = dict(com)
        m["xTo"] = np.ascontiguousarray(xT[:, own])
        m["xTf"] = np.ascontiguousarray(xT[:, fidx])
        m["cosbo"] = np.ascontiguousarray(np.broadcast_to(cw[own], (128, S)))
        m["sinbo"] = np.ascontiguousarray(np.broadcast_to(sw[own], (128, S)))
        m["cost"] = np.ascontiguousarray(cw[chunk_pos].reshape(-1, 128).T)
        m["sint"] = np.ascontiguousarray(sw[chunk_pos].reshape(-1, 128).T)
        maps.append(m)
    return maps


def gather_out(cfg, results):
    NO = cfg["NO"]
    S = NO * 128
    L = 8 * S
    outs = [np.asarray(r["outT"], dtype=np.float32) for r in results]
    full = np.concatenate(outs, axis=1)
    return np.ascontiguousarray(full.T).reshape(1, L, DM)


def kernel(L0_attn_x, L0_attn_W_qkv, L0_attn_b_qkv, L0_attn_W_out, L0_attn_b_out,
           L0_ffn_gate_w, L0_ffn_ws_in, L0_ffn_vs_in, L0_ffn_ws_out,
           L0_ffn_e_win, L0_ffn_e_vin, L0_ffn_e_wout):
    cfg = REAL
    A = dict(x=np.asarray(L0_attn_x), W_qkv=np.asarray(L0_attn_W_qkv),
             b_qkv=np.asarray(L0_attn_b_qkv), W_out=np.asarray(L0_attn_W_out),
             b_out=np.asarray(L0_attn_b_out), gate_w=np.asarray(L0_ffn_gate_w),
             ws_in=np.asarray(L0_ffn_ws_in), vs_in=np.asarray(L0_ffn_vs_in),
             ws_out=np.asarray(L0_ffn_ws_out), e_win=np.asarray(L0_ffn_e_win),
             e_vin=np.asarray(L0_ffn_e_vin), e_wout=np.asarray(L0_ffn_e_wout))
    nc = build(cfg)
    in_maps = make_in_maps(cfg, A)
    res = run_bass_kernel_spmd(nc, in_maps, core_ids=list(range(8)))
    return gather_out(cfg, res.results)


# version 10
# speedup vs baseline: 1.0143x; 1.0143x over previous
import numpy as np
import ml_dtypes
import concourse.bass as bass
import concourse.mybir as mybir
from concourse.bass_utils import run_bass_kernel_spmd

F32 = mybir.dt.float32
BF16 = mybir.dt.bfloat16
I32 = mybir.dt.int32
AF = mybir.ActivationFunctionType
ALU = mybir.AluOpType
X = mybir.AxisListType.X
ds = bass.ds

MAGIC = 77001.0
MAGIC_I = int(np.float32(MAGIC).view(np.int32))
POLL_LIMIT = 30000

H, DM, KCH = 16, 1024, 8
NEXP, HIDC = 18, 4
EPS_LN, EPS_N = 1e-5, 1e-4
G = 256

REAL = dict(NO=16)


class Prog:
    ENG = ("sync", "gpsimd", "tensor", "scalar", "vector")

    def __init__(self):
        self.t = {e: [] for e in self.ENG}

    def add(self, eng, fn):
        self.t[eng].append(fn)


class Q:
    def __init__(self, prog, eng, sem, step=1):
        self.p, self.eng, self.sem, self.step, self.n = prog, eng, sem, step, 0

    def do(self, fn, inc=True):
        if inc:
            self.n += self.step
            sem, st = self.sem, self.step
            self.p.add(self.eng, lambda e, fn=fn, sem=sem, st=st: fn(e).then_inc(sem, st))
        else:
            self.p.add(self.eng, lambda e, fn=fn: fn(e))
        return self.n

    def wait(self, q2, val=None):
        v = q2.n if val is None else val
        if v <= 0:
            return
        sem = q2.sem
        self.p.add(self.eng, lambda e, sem=sem, v=v: e.wait_ge(sem, v))


def build(cfg, exchange="poll", detect_races=True):
    NO = cfg["NO"]
    NF = 3 * NO
    S = NO * 128
    FS = NF * 128
    NC = NO + NF
    n_og, n_fg = NO // 2, NF // 2
    n_ag = S // G
    FH = S // 2
    FSUB = min(512, FH)
    nsub = FH // FSUB

    nc = bass.Bass("TRN2", target_bir_lowering=False,
                   detect_race_conditions=detect_races)

    xTo = nc.dram_tensor("xTo", [DM, S], F32, kind="ExternalInput")
    xTf = nc.dram_tensor("xTf", [DM, FS], F32, kind="ExternalInput")
    wqkv = nc.dram_tensor("wqkv", [DM, 3 * DM], F32, kind="ExternalInput")
    wout = nc.dram_tensor("wout", [DM, DM], F32, kind="ExternalInput")
    gwt = nc.dram_tensor("gwt", [128, 128], F32, kind="ExternalInput")
    wsin = nc.dram_tensor("wsin", [DM, DM], F32, kind="ExternalInput")
    wvin = nc.dram_tensor("wvin", [DM, DM], F32, kind="ExternalInput")
    wsout = nc.dram_tensor("wsout", [DM, DM], F32, kind="ExternalInput")
    ewin = nc.dram_tensor("ewin", [16, DM, 512], F32, kind="ExternalInput")
    evin = nc.dram_tensor("evin", [16, DM, 512], F32, kind="ExternalInput")
    ewout = nc.dram_tensor("ewout", [16, 512, DM], F32, kind="ExternalInput")
    bqt = nc.dram_tensor("bqt", [128, 8], F32, kind="ExternalInput")
    bkv = nc.dram_tensor("bkv", [1, 2048], F32, kind="ExternalInput")
    bot = nc.dram_tensor("bot", [128, 8], F32, kind="ExternalInput")
    cosbo = nc.dram_tensor("cosbo", [128, S], F32, kind="ExternalInput")
    sinbo = nc.dram_tensor("sinbo", [128, S], F32, kind="ExternalInput")
    cost = nc.dram_tensor("cost", [128, NC], F32, kind="ExternalInput")
    sint = nc.dram_tensor("sint", [128, NC], F32, kind="ExternalInput")
    identin = nc.dram_tensor("identin", [128, 128], F32, kind="ExternalInput")
    outT = nc.dram_tensor("outT", [DM, S], F32, kind="ExternalOutput")
    sdata = nc.dram_tensor("sdata", [128, 2080], F32, kind="Internal", addr_space="Shared")
    sflag = nc.dram_tensor("sflag", [1, 16], F32, kind="Internal", addr_space="Shared")
    cc_in = nc.dram_tensor("cc_in", [128, 1040], F32, kind="Internal")
    cc_out = nc.dram_tensor("cc_out", [128, 1040], F32, kind="Internal")

    from contextlib import ExitStack
    with ExitStack() as ctx:
        sb = lambda n, s, d: ctx.enter_context(nc.sbuf_tensor(n, s, d))
        AR = sb("AR", [128, KCH, S], F32)
        AX = sb("AX", [128, KCH, S], BF16)
        AQ = sb("AQ", [128, KCH, S], BF16)
        AW = sb("AW", [128, 24576], BF16)
        AC = sb("AC", [128, 4096], BF16)
        AK = sb("AK", [128, 3136], BF16)
        AE = sb("AE", [128, 2080], F32)
        RBF = sb("RBF", [1, 1024], BF16)
        consts = sb("consts", [128, 8], F32)
        bq_t = sb("bq_t", [128, 8], F32)
        bo_t = sb("bo_t", [128, 8], F32)
        gwTb = sb("gwTb", [128, 128], BF16)
        cosTt = sb("cosTt", [128, NC], F32)
        sinTt = sb("sinTt", [128, NC], F32)
        identb = sb("identb", [128, 128], BF16)
        combT = sb("combT", [128, 16 * NO], BF16)
        crow = sb("crow", [1, 512], BF16)
        ones = sb("ones", [128, 128], BF16)
        xbt = sb("xbt", [128, 1024], BF16)
        LNB = sb("LNB", [128, 1024], BF16)
        LNT = sb("LNT", [128, 256], BF16)
        magic = sb("magic", [1, 8], F32)
        flg = sb("flg", [1, 16], F32)
        pb = [ctx.enter_context(nc.psum_tensor(f"pb{i}", [128, 512], F32))
              for i in range(8)]
        sems = {}
        for sname in ("sXL", "sTB", "sWKV", "sCF", "sXF0", "sXF1", "sWQ",
                      "sWO2", "sWV0", "sWV1", "sWB0", "sWB1", "sZF", "sXW",
                      "sFW", "sPL", "sPR", "sH2", "sPE", "sDVE", "sACT", "sSX"):
            sems[sname] = ctx.enter_context(nc.semaphore(sname))

        prog = Prog()
        ql_x = Q(prog, "sync", sems["sXL"], 16)
        ql_tb = Q(prog, "sync", sems["sTB"], 16)
        ql_wkv = Q(prog, "gpsimd", sems["sWKV"], 16)
        ql_cf = Q(prog, "gpsimd", sems["sCF"], 16)
        ql_xf = [Q(prog, "gpsimd", sems["sXF0"], 16),
                 Q(prog, "gpsimd", sems["sXF1"], 16)]
        ql_wq = Q(prog, "gpsimd", sems["sWQ"], 16)
        ql_wo2 = Q(prog, "gpsimd", sems["sWO2"], 16)
        ql_wv = [Q(prog, "gpsimd", sems["sWV0"], 16),
                 Q(prog, "gpsimd", sems["sWV1"], 16)]
        ql_wb = [Q(prog, "gpsimd", sems["sWB0"], 16),
                 Q(prog, "gpsimd", sems["sWB1"], 16)]
        ql_zf = Q(prog, "sync", sems["sZF"], 16)
        ql_xw = Q(prog, "sync", sems["sXW"], 16)
        ql_fw = Q(prog, "sync", sems["sFW"], 16)
        ql_pr = Q(prog, "sync", sems["sPR"], 16)
        h2 = Q(prog, "sync", sems["sH2"], 16)
        pe = Q(prog, "tensor", sems["sPE"])
        dv = Q(prog, "vector", sems["sDVE"])
        ac = Q(prog, "scalar", sems["sACT"])
        sx = Q(prog, "sync", sems["sSX"])

        W1 = AW[:, 0:8192]

        W2 = AW[:, 8192:16384]
        W3 = AW[:, 16384:20480]
        W4 = AW[:, 20480:24576]
        kcs2 = AK[:, 0:2048].rearrange("p (h x) -> p h x", h=16)
        vt2 = AK[:, 2048:3088].rearrange("p (h x) -> p h x", h=16)
        TMPA = AK[:, 0:1024].bitcast(F32)[:, 0:512]
        TMPB = AK[:, 1024:2048].bitcast(F32)[:, 0:512]

        bkv_r = AW[0:1, 20480:22528]
        RSTD_B = None
        CEPS = consts[:, 0:1]
        CNH = consts[:, 1:2]
        CM1 = consts[:, 3:4]

        ql_x.do(lambda e: e.dma_start(
            out=AR[:, :, :], in_=xTo[:, :].rearrange("(c p) t -> p c t", p=128)))
        for t, src in ((bq_t, bqt), (bo_t, bot), (cosTt, cost), (sinTt, sint)):
            ql_tb.do(lambda e, t=t, src=src: e.dma_start(out=t[:, :], in_=src[:, :]))

        ql_wkv.do(lambda e: e.dma_start(
            out=AW[:, 0:16384].rearrange("p (c f) -> p c f", c=8),
            in_=wqkv[:, 1024:3072].rearrange("(c p) f -> p c f", p=128)))
        ql_cf.do(lambda e: e.dma_start(out=W3[:, 0:S], in_=cosbo[:, :]))
        ql_cf.do(lambda e: e.dma_start(out=W3[:, 2048:2048 + S], in_=sinbo[:, :]))
        ql_cf.do(lambda e: e.dma_start(out=gwTb[:, :], in_=gwt[:, :]))
        ql_cf.do(lambda e: e.dma_start(out=bkv_r[:, :], in_=bkv[:, :]))
        ql_cf.do(lambda e: e.dma_start(out=identb[:, :], in_=identin[:, :]))

        dv.do(lambda e: e.memset(ones[:, :], 1.0))
        dv.do(lambda e: e.memset(consts[:, 0:1], EPS_LN))
        dv.do(lambda e: e.memset(consts[:, 1:2], -0.5))
        dv.do(lambda e: e.memset(consts[:, 2:3], EPS_N))
        dv.do(lambda e: e.memset(consts[:, 3:4], -1.0))
        dv.do(lambda e: e.memset(vt2[:, :, 64:65], 1.0))
        dv.do(lambda e: e.memset(magic[0:1, 0:1], MAGIC))
        dv.do(lambda e: e.memset(magic[0:1, 1:2], 0.0))
        m_consts = dv.n

        ql_zf.wait(dv, m_consts)

        def _zeroflag(e):
            slot = e.partition_id() % 2
            e.dma_start(out=sflag[0:1, ds(slot, 1)], in_=magic[0:1, 1:2])\
                .then_inc(sems["sZF"], 16)
        ql_zf.n += 16
        prog.add("sync", _zeroflag)

        WKV = AW[:, 0:16384].rearrange("p (c f) -> p c f", c=8)
        COSB = W3[:, 0:2048]
        SINB = W3[:, 2048:4096]

        ROW_A = AE[0:1, 0:512]
        ROW_B = AE[0:1, 512:1024]
        RB_STD = RBF[0:1, 0:512]
        RB_NMR = RBF[0:1, 512:1024]
        RSTD_B = LNB[:, 0:512]
        NMR_B = LNB[:, 512:1024]

        def emit_ln(src, cast_needed, apply, extra_waits=None, W=256,
                    xa=None, xs=None):
            pe.wait(dv, emit_ln.last_lnb)
            pe.wait(ac, emit_ln.last_rstd)
            if extra_waits:
                extra_waits()
            m_sq = dict(emit_ln.prev_msq)
            for k in range(KCH):
                pp = k % 2
                if cast_needed:
                    xb = xa[:, pp * W:(pp + 1) * W]
                    sq = xs[:, pp * W:(pp + 1) * W]
                else:
                    sq = xbt[:, pp * 512:pp * 512 + W]
                if cast_needed:
                    if (k - 2) in m_sq:
                        dv.wait(pe, m_sq[k - 2][1])
                    dv.do(lambda e, xb=xb, k=k: e.tensor_copy(xb, src(k)))
                    m_cast = dv.n
                    ac.wait(dv, m_cast)
                    if (k - 2) in m_sq:
                        ac.wait(pe, m_sq[k - 2][1])
                    ac.do(lambda e, sq=sq, xb=xb: e.activation(sq, xb, AF.Square))
                    msq_a = ac.n
                    pe.wait(dv, m_cast)
                    xin = xb
                else:
                    if (k - 2) in m_sq:
                        ac.wait(pe, m_sq[k - 2][1])
                    ac.do(lambda e, sq=sq, k=k: e.activation(sq, src(k), AF.Square))
                    msq_a = ac.n
                    xin = None
                pe.wait(ac, msq_a)
                st, sp = (k == 0), (k == KCH - 1)
                if xin is not None:
                    pe.do(lambda e, xin=xin, st=st, sp=sp: e.matmul(
                        pb[6][0:1, 0:W], ones[:, 0:1], xin, start=st, stop=sp))
                else:
                    pe.do(lambda e, k=k, st=st, sp=sp: e.matmul(
                        pb[6][0:1, 0:W], ones[:, 0:1], src(k), start=st, stop=sp))
                pe.do(lambda e, sq=sq, st=st, sp=sp: e.matmul(
                    pb[7][0:1, 0:W], ones[:, 0:1], sq, start=st, stop=sp))
                m_sq[k] = (pe.n - 1, pe.n)
            m_stats = pe.n
            dv.wait(pe, m_stats)
            dv.do(lambda e: e.tensor_scalar(ROW_A[:, 0:W], pb[6][0:1, 0:W],
                                            1.0 / DM, None, ALU.mult))
            dv.do(lambda e: e.tensor_scalar(ROW_B[:, 0:W], pb[7][0:1, 0:W],
                                            1.0 / DM, None, ALU.mult))
            m_mq = dv.n
            dv.wait(dv, m_mq)
            dv.do(lambda e: e.tensor_tensor(pb[6][0:1, 0:W], ROW_A[:, 0:W],
                                            ROW_A[:, 0:W], ALU.mult))
            m_msq = dv.n
            dv.wait(dv, m_msq)
            dv.do(lambda e: e.tensor_tensor(ROW_B[:, 0:W], ROW_B[:, 0:W],
                                            pb[6][0:1, 0:W], ALU.subtract))
            m_var = dv.n
            ac.wait(dv, m_var)
            ac.do(lambda e: e.activation(pb[7][0:1, 0:W], ROW_B[:, 0:W],
                                         AF.Ln, bias=CEPS[0:1, :]))
            m_l = ac.n
            ac.wait(ac, m_l)
            ac.do(lambda e: e.activation(RB_STD[0:1, 0:W], pb[7][0:1, 0:W],
                                         AF.Exp, scale=CNH[0:1, :]))
            m_rstd = ac.n
            emit_ln.last_rstd = m_rstd
            dv.wait(ac, m_rstd)
            dv.do(lambda e: e.tensor_tensor(RB_NMR[0:1, 0:W], ROW_A[:, 0:W],
                                            RB_STD[0:1, 0:W], ALU.mult))
            m_rows = dv.n
            pe.wait(dv, m_rows)
            pe.do(lambda e: e.matmul(pb[6][:, 0:W], ones[0:1, :],
                                     RB_STD[0:1, 0:W], start=True, stop=True))
            pe.do(lambda e: e.matmul(pb[7][:, 0:W], ones[0:1, :],
                                     RB_NMR[0:1, 0:W], start=True, stop=True))
            m_bcast = pe.n
            dv.wait(pe, m_bcast)
            dv.do(lambda e: e.tensor_copy(LNB[:, 0:W], pb[6][:, 0:W]))
            dv.do(lambda e: e.tensor_copy(LNB[:, 512:512 + W], pb[7][:, 0:W]))
            m_lnb = dv.n
            emit_ln.last_lnb = m_lnb
            emit_ln.prev_msq = {k - KCH: v for k, v in m_sq.items() if k >= KCH - 2}
            apply()
            return dv.n
        emit_ln.last_lnb = 0
        emit_ln.last_rstd = 0
        emit_ln.prev_msq = {}

        def apply_to_bf16(dst_of, src_of, W=256):
            def _a():
                for k in range(KCH):
                    for j in range(W // 256):
                        o = j * 256
                        dv.wait(dv, dv.n)
                        dv.do(lambda e, k=k, o=o: e.tensor_tensor(
                            LNT[:, 0:256], src_of(k)[:, o:o + 256],
                            RSTD_B[:, o:o + 256], ALU.mult))
                        mm = dv.n
                        dv.wait(dv, mm)
                        dv.do(lambda e, k=k, o=o: e.tensor_tensor(
                            dst_of(k)[:, o:o + 256], LNT[:, 0:256],
                            NMR_B[:, o:o + 256], ALU.subtract))
            return _a

        def apply_residual(src_of, res_of, W=256):
            def _a():
                for k in range(KCH):
                    for j in range(W // 256):
                        o = j * 256
                        dv.wait(dv, dv.n)
                        dv.do(lambda e, k=k, o=o: e.tensor_tensor(
                            LNT[:, 0:256], src_of(k)[:, o:o + 256],
                            RSTD_B[:, o:o + 256], ALU.mult))
                        mm = dv.n
                        dv.wait(dv, mm)
                        dv.do(lambda e, k=k, o=o: e.tensor_tensor(
                            LNT[:, 0:256], LNT[:, 0:256],
                            NMR_B[:, o:o + 256], ALU.subtract))
                        mm2 = dv.n
                        dv.wait(dv, mm2)
                        dv.do(lambda e, k=k, o=o: e.tensor_tensor(
                            res_of(k)[:, o:o + 256], res_of(k)[:, o:o + 256],
                            LNT[:, 0:256], ALU.add))
            return _a

        XFP = [AQ[:, :, b * 512:(b + 1) * 512] for b in range(min(4, S // 512))]
        NBUF = len(XFP)
        assert NBUF >= 2, "need at least 1024 own tokens per core"
        n_og4 = NO // 4
        n_fg4 = (3 * NO) // 4
        NKVG = n_og4 + n_fg4

        xf_marks = {}
        for fgi in range(min(2, n_fg4)):
            ql_xf[fgi % 2].do(lambda e, fgi=fgi: e.dma_start(
                out=XFP[fgi % NBUF][:, :, :],
                in_=xTf[:, fgi * 512:(fgi + 1) * 512]
                .rearrange("(c p) t -> p c t", p=128)))
            xf_marks[fgi] = (ql_xf[fgi % 2], ql_xf[fgi % 2].n)

        kv_bank_marks = {}
        pe.wait(ql_x)
        pe.wait(ql_tb)
        pe.wait(dv, m_consts)
        pe.wait(ql_wkv)
        pe.wait(ql_cf)
        ac.wait(dv, m_consts)
        ac.wait(ql_tb)
        dv.wait(ql_tb)
        dv.wait(ql_x)

        kvproj_done = {}

        def emit_kvacc(c, m_ev_d, m_ev_a, heads=range(H)):
            pe.wait(dv, m_ev_d)
            pe.wait(ac, m_ev_a)
            for h in heads:
                bsel = 0 if h < 7 else (1 if h < 14 else 2)
                accb = pb[2 + bsel]
                col = 65 * (h - (0, 7, 14)[bsel])
                fc = (c == 0) and (h in (0, 7, 14))
                lc = (c == NC - 1) and (h in (6, 13, 15))
                pe.do(lambda e, h=h, accb=accb, col=col, fc=fc, lc=lc:
                      e.matmul(accb[:, col:col + 65], kcs2[:, h, :],
                               vt2[:, h, 0:65],
                               start=fc, stop=lc), inc=(h == H - 1))
            emit_kvacc.last = pe.n
        emit_kvacc.last = 0
        emit_kvacc.pending = []

        def ln_for_group(g):
            own = g < n_og4
            fg = g - n_og4
            if own:
                src = lambda k, g=g: AR[:, k, g * 512:(g + 1) * 512]
                extra = None
                apply = apply_to_bf16(
                    lambda k, g=g: AX[:, k, g * 512:(g + 1) * 512],
                    lambda k, g=g: AR[:, k, g * 512:(g + 1) * 512], W=512)
                xa, xs = AC[:, 0:1024], AC[:, 1024:2048]
            else:
                xfv = XFP[fg % NBUF]
                src = lambda k, xfv=xfv: xfv[:, k, 0:512]
                lq, lv = xf_marks[fg]
                extra = lambda lq=lq, lv=lv: (pe.wait(lq, lv), ac.wait(lq, lv),
                                              dv.wait(lq, lv))
                apply = apply_to_bf16(
                    lambda k, xfv=xfv: xfv[:, k, 0:512],
                    lambda k, xfv=xfv: xfv[:, k, 0:512], W=512)
                xa = xs = None
            return emit_ln(src, cast_needed=own, apply=apply, extra_waits=extra,
                           W=512, xa=xa, xs=xs)

        apply_marks = {0: ln_for_group(0)}
        for g in range(NKVG):
            own = g < n_og4
            fg = g - n_og4
            m_apply = apply_marks[g]

            if own:
                xn_of = lambda k, ci, g=g: AX[:, k, (g * 4 + ci) * 128:
                                              (g * 4 + ci) * 128 + 128]
            else:
                xn_of = lambda k, ci, xfv=XFP[fg % NBUF]: \
                    xfv[:, k, ci * 128:ci * 128 + 128]

            for ci in range(4):
                c = g * 4 + ci
                nmarks = {}
                for n in (0, 2, 1, 3):
                    bi3 = (ci * 4 + (0, 2, 1, 3).index(n)) % 3
                    bank = pb[(0, 1, 5)[bi3]]
                    prev = kv_bank_marks.get(bi3)
                    if prev is not None:
                        pe.wait(dv, prev[0])
                        pe.wait(ac, prev[1])
                    pe.wait(dv, m_apply)
                    for k in range(KCH):
                        pe.do(lambda e, k=k, ci=ci, n=n, bank=bank, xn_of=xn_of:
                              e.matmul(bank[:, 0:512], xn_of(k, ci),
                                       WKV[:, k, n * 512:(n + 1) * 512],
                                       start=(k == 0), stop=False), inc=False)
                    pe.do(lambda e, n=n, bank=bank: e.matmul(
                        bank[:, 0:512], ones[0:1, :],
                        bkv_r[0:1, n * 512:(n + 1) * 512],
                        start=False, stop=True))
                    m_proj = pe.n
                    if n < 2:
                        dv.wait(pe, m_proj)
                        dv.wait(pe, emit_kvacc.last)
                        bview = bank[:, 0:512].rearrange("p (h x) -> p h x", h=8)
                        dv.do(lambda e, bview=bview, n=n, c=c: e.tensor_scalar(
                            kcs2[:, 8 * n:8 * n + 8, 0:64], bview, 0.0,
                            cosTt[:, c:c + 1], ALU.max, ALU.mult))
                        dv.do(lambda e, bview=bview, n=n, c=c: e.tensor_scalar(
                            kcs2[:, 8 * n:8 * n + 8, 64:128], bview, 0.0,
                            sinTt[:, c:c + 1], ALU.max, ALU.mult))
                        kv_bank_marks[bi3] = (dv.n, ac.n)
                    else:
                        ac.wait(pe, m_proj)
                        ac.wait(pe, emit_kvacc.last)
                        bv2 = bank[:, 0:512].rearrange("p (h x) -> p h x", h=8)
                        ac.do(lambda e, bv2=bv2, n=n: e.copy(
                            vt2[:, 8 * (n - 2):8 * (n - 2) + 8, 0:64], bv2))
                        kv_bank_marks[bi3] = (dv.n, ac.n)
                    nmarks[n] = (dv.n, ac.n)
                    if n == 2:
                        emit_kvacc(c, nmarks[0][0], nmarks[2][1], range(0, 8))
                emit_kvacc(c, nmarks[1][0], nmarks[3][1], range(8, H))
            kvproj_done[g] = pe.n

            if g + 1 < NKVG:
                apply_marks[g + 1] = ln_for_group(g + 1)

            tgt = (g + 2) - n_og4
            if 2 <= tgt < n_fg4 and tgt not in xf_marks:
                qq = ql_xf[tgt % 2]
                qq.wait(pe, kvproj_done[g])
                qq.do(lambda e, tgt=tgt, b=tgt % NBUF: e.dma_start(
                    out=XFP[b][:, :, :],
                    in_=xTf[:, tgt * 512:(tgt + 1) * 512]
                    .rearrange("(c p) t -> p c t", p=128)))
                xf_marks[tgt] = (qq, qq.n)
        m_kv_mms = pe.n

        kvloc = AE[:, 0:1040]
        partner = AE[:, 1040:2080]
        dv.wait(pe, m_kv_mms)
        dv.do(lambda e: e.tensor_copy(kvloc[:, 0:455], pb[2][:, 0:455]))
        dv.do(lambda e: e.tensor_copy(kvloc[:, 455:910], pb[3][:, 0:455]))
        dv.do(lambda e: e.tensor_copy(kvloc[:, 910:1040], pb[4][:, 0:130]))
        m_kvloc = dv.n

        if exchange == "poll":
            ql_xw.wait(ql_zf)
            ql_xw.wait(dv, m_kvloc)

            def _exchange(e):
                off = (e.partition_id() % 2) * 1040
                e.dma_start(out=sdata[:, ds(off, 1040)], in_=kvloc)\
                    .then_inc(sems["sXW"], 16)
            ql_xw.n += 16
            prog.add("sync", _exchange)
            ql_fw.wait(ql_xw)

            def _flagwrite(e):
                slot = e.partition_id() % 2
                e.dma_start(out=sflag[0:1, ds(slot, 1)], in_=magic[0:1, 0:1])\
                    .then_inc(sems["sFW"], 16)
            ql_fw.n += 16
            prog.add("sync", _flagwrite)

            def _poll(e):
                e.wait_ge(sems["sFW"], 16)
                with e.register("more") as more, e.register("cnt") as cnt, \
                        e.register("it") as it, e.register("exp") as exp:
                    e.reg_mov(more, 1)
                    e.reg_mov(it, 0)
                    e.reg_mov(exp, 0)
                    with e.While(more):
                        e.reg_add(it, it, 1)
                        e.dma_start(out=flg[:, :], in_=sflag[:, :])\
                            .then_inc(sems["sPL"], 16)
                        e.reg_add(exp, exp, 16)
                        e.wait_ge(sems["sPL"], exp)
                        e.reg_mov(cnt, 0)
                        for i in range(2):
                            fv = e.value_load(flg[0:1, i:i + 1].bitcast(I32))
                            with e.register(f"eq{i}") as req:
                                e.reg_alu(req, fv, MAGIC_I, ALU.is_equal)
                                e.reg_alu(cnt, cnt, req, ALU.add)
                        e.reg_alu(more, cnt, 2, ALU.is_lt)
                        with e.register("lim") as lim:
                            e.reg_alu(lim, it, POLL_LIMIT, ALU.is_lt)
                            e.reg_alu(more, more, lim, ALU.bitwise_and)
                    poff = (1 - (e.partition_id() % 2)) * 1040
                    e.dma_start(out=partner, in_=sdata[:, ds(poff, 1040)])\
                        .then_inc(sems["sPR"], 16)
            prog.add("sync", _poll)
            ql_pr.n += 16
            sx.wait(ql_pr)
            prog.add("sync", lambda e: e.sem_inc(sems["sSX"], 1))
            sx.n += 1
            m_xdone = sx.n
        else:
            ql_xw.wait(dv, m_kvloc)
            ql_xw.do(lambda e: e.dma_start(out=cc_in[:, :], in_=kvloc))
            prog.add("gpsimd", lambda e: e.wait_ge(sems["sXW"], ql_xw.n))
            prog.add("gpsimd", lambda e: e.collective_compute(
                "AllReduce", ALU.add,
                replica_groups=[[0, 1], [2, 3], [4, 5], [6, 7]],
                ins=[cc_in[:, :].opt()],
                outs=[cc_out[:, :].opt()]).then_inc(sems["sPR"], 16))
            ql_pr.n += 16
            prog.add("sync", lambda e, v=ql_pr.n: e.wait_ge(sems["sPR"], v))
            ql_fw.do(lambda e: e.dma_start(out=kvloc, in_=cc_out[:, :]))
            dv.do(lambda e: e.memset(partner, 0.0))
            sx.wait(ql_fw)
            prog.add("sync", lambda e: e.sem_inc(sems["sSX"], 1))
            sx.n += 1
            m_xdone = sx.n

        dv.wait(sx, m_xdone)
        dv.wait(dv, dv.n)
        dv.do(lambda e: e.tensor_tensor(kvloc, kvloc, partner, ALU.add))
        m_sum = dv.n
        dv.wait(dv, m_sum)
        stats_bf = partner.bitcast(BF16)
        SB_COS = stats_bf[0:64, 0:1040]
        SB_SIN = stats_bf[0:64, 1040:2080]
        dv.do(lambda e: e.tensor_copy(SB_COS, kvloc[0:64, :]))
        dv.do(lambda e: e.tensor_copy(SB_SIN, kvloc[64:128, :]))
        m_stats_bf = dv.n

        ql_wq.wait(pe, m_kv_mms)
        ql_wq.do(lambda e: e.dma_start(
            out=W1[:, :].rearrange("p (c f) -> p c f", c=8),
            in_=wqkv[:, 0:1024].rearrange("(c p) f -> p c f", p=128)))
        ql_wo2.wait(pe, m_kv_mms)
        ql_wo2.do(lambda e: e.dma_start(
            out=W2[:, :].rearrange("p (c f) -> p c f", c=8),
            in_=wout[:, :].rearrange("(c p) f -> p c f", p=128)))
        WQ = W1[:, :].rearrange("p (c f) -> p c f", c=8)
        WO = W2[:, :].rearrange("p (c f) -> p c f", c=8)

        QR = [W4[0:64, 0:256], W4[0:64, 256:512]]
        QC = [W4[0:64, 512:768], W4[0:64, 768:1024]]
        QS = [W4[0:64, 1024:1280], W4[0:64, 1280:1536]]
        CS = [W4[:, 1536:2048].bitcast(F32)[0:64, 0:256],
              W4[:, 2048:2560].bitcast(F32)[0:64, 0:256]]
        CSROW = [W4[:, 1536:2048].bitcast(F32)[0:1, 0:256],
                 W4[:, 2048:2560].bitcast(F32)[0:1, 0:256]]
        RNR = [W4[0:1, 2560:2816], W4[0:1, 2816:3072]]

        pe.wait(ql_wq)
        pe.wait(dv, m_stats_bf)
        ac.wait(dv, m_stats_bf)
        NIT = n_ag * H
        M = [dict() for _ in range(NIT)]

        def hg(it):
            return it // H, it % H

        def emit_q(it):
            g, h = hg(it)
            pp = it % 2
            qb = pb[4 + pp]
            bq_col = bq_t[64 * (h % 2):64 * (h % 2) + 64, h // 2:h // 2 + 1]
            if it >= 2:
                pe.wait(ac, M[it - 2]["qr"])
            for k in range(KCH):
                pe.do(lambda e, k=k, g=g, h=h, qb=qb: e.matmul(
                    qb[0:64, 0:256], WQ[:, k, 64 * h:64 * h + 64],
                    AX[:, k, g * 256:(g + 1) * 256],
                    start=(k == 0), stop=(k == KCH - 1)),
                    inc=(k == KCH - 1))
            M[it]["q"] = pe.n
            ac.wait(pe, M[it]["q"])
            if it >= 2:
                ac.wait(dv, M[it - 2]["qcs"])
            ac.do(lambda e, qb=qb, pp=pp, bq_col=bq_col: e.activation(
                QR[pp], qb[0:64, 0:256], AF.Relu, bias=bq_col))
            M[it]["qr"] = ac.n
            dv.wait(ac, M[it]["qr"])
            if it >= 2:
                dv.wait(pe, M[it - 2]["ctx"])
            dv.do(lambda e, pp=pp, g=g: e.tensor_tensor(
                QC[pp], QR[pp], COSB[0:64, g * 256:(g + 1) * 256], ALU.mult))
            dv.do(lambda e, pp=pp, g=g: e.tensor_tensor(
                QS[pp], QR[pp], SINB[0:64, g * 256:(g + 1) * 256], ALU.mult))
            M[it]["qcs"] = dv.n

        def emit_ctx(it):
            g, h = hg(it)
            pp = it % 2
            cb = pb[pp]
            pe.wait(dv, M[it]["qcs"])
            if it >= 2:
                pe.wait(ac, M[it - 2]["cs"])
            pe.do(lambda e, h=h, pp=pp, cb=cb: e.matmul(
                cb[0:65, 0:256], SB_COS[:, 65 * h:65 * h + 65], QC[pp],
                start=True, stop=False), inc=False)
            pe.do(lambda e, h=h, pp=pp, cb=cb: e.matmul(
                cb[0:65, 0:256], SB_SIN[:, 65 * h:65 * h + 65], QS[pp],
                start=False, stop=True))
            M[it]["ctx"] = pe.n
            ac.wait(pe, M[it]["ctx"])
            if it >= 2:
                ac.wait(dv, M[it - 2]["tt"])
            ac.do(lambda e, cb=cb, pp=pp: e.activation(
                CSROW[pp], cb[64:65, 0:256], AF.Ln, bias=consts[64:65, 2:3]))
            m_l = ac.n
            ac.wait(ac, m_l)
            if it >= 2:
                ac.wait(pe, M[it - 2]["nb"])
            ac.do(lambda e, pp=pp: e.activation(
                RNR[pp][0:1, 0:256], CSROW[pp], AF.Exp,
                scale=consts[0:1, 3:4]))
            M[it]["rn"] = ac.n
            ac.wait(ac, M[it]["rn"])
            ac.do(lambda e, cb=cb, pp=pp: e.copy(CS[pp], cb[0:64, 0:256]))
            M[it]["cs"] = ac.n

        def emit_nb(it):
            g, h = hg(it)
            pp = it % 2
            nb = pb[2 + pp]
            pe.wait(ac, M[it]["rn"])
            if it >= 2:
                pe.wait(dv, M[it - 2]["tt"])
            pe.do(lambda e, pp=pp, nb=nb: e.matmul(
                nb[0:64, 0:256], ones[0:1, 0:64], RNR[pp][0:1, 0:256],
                start=True, stop=True))
            M[it]["nb"] = pe.n
            dv.wait(pe, M[it]["nb"])
            dv.wait(ac, M[it]["cs"])
            ro = 64 * (h % 2)
            dv.do(lambda e, h=h, g=g, pp=pp, nb=nb, ro=ro: e.tensor_tensor(
                AQ[ro:ro + 64, h // 2, g * 256:(g + 1) * 256],
                CS[pp], nb[0:64, 0:256], ALU.mult))
            M[it]["tt"] = dv.n

        for it in range(NIT + 2):
            if it < NIT:
                emit_q(it)
            if 0 <= it - 1 < NIT:
                emit_ctx(it - 1)
            if 0 <= it - 2 < NIT:
                emit_nb(it - 2)
        m_ctx_done = dv.n

        gT = AK[:, 0:64].bitcast(F32)[:, 0:16]
        top8 = AK[:, 64:128].bitcast(F32)[:, 0:8]
        msk = AK[:, 128:192].bitcast(F32)[:, 0:16]
        gsum = AK[:, 192:200].bitcast(F32)[:, 0:1]
        combine_F = partner.bitcast(BF16)[0:16, 0:2048]
        gate_state = {"m_sig": 0}

        ln2_apply = {}

        def emit_ln2(g):
            emit_ln(lambda k, g=g: AR[:, k, g * 256:(g + 1) * 256],
                    cast_needed=True,
                    apply=apply_to_bf16(
                        lambda k, g=g: AX[:, k, g * 256:(g + 1) * 256],
                        lambda k, g=g: AR[:, k, g * 256:(g + 1) * 256]),
                    W=256, xa=xbt[:, 0:512], xs=xbt[:, 512:1024])
            ln2_apply[g] = dv.n
            return dv.n

        def emit_gate(c):
            if gate_state["m_sig"]:
                pe.wait(ac, gate_state["m_sig"])
            pe.wait(dv, ln2_apply[c // 2])
            pe.wait(dv, emit_ln.last_lnb)
            for k in range(KCH):
                pe.do(lambda e, k=k, c=c: e.matmul(
                    pb[6][0:128, 0:16], AX[:, k, c * 128:(c + 1) * 128],
                    gwTb[:, 16 * k:16 * k + 16],
                    start=(k == 0), stop=(k == KCH - 1)), inc=(k == KCH - 1))
            m_g = pe.n
            ac.wait(pe, m_g)
            ac.wait(dv, dv.n)
            ac.do(lambda e: e.activation(gT, pb[6][:, 0:16], AF.Sigmoid))
            m_sig = ac.n
            gate_state["m_sig"] = m_sig
            dv.wait(ac, m_sig)
            dv.do(lambda e: e.max(out=top8, in_=gT))
            dv.wait(dv, dv.n)
            dv.do(lambda e: e.tensor_scalar(msk, gT, top8[:, 3:4], None, ALU.is_ge))
            dv.wait(dv, dv.n)
            dv.do(lambda e: e.tensor_tensor(msk, msk, gT, ALU.mult))
            dv.wait(dv, dv.n)
            dv.do(lambda e: e.reduce_sum(gsum, msk, axis=X))
            dv.wait(dv, dv.n)
            dv.do(lambda e: e.reciprocal(gsum, gsum))
            dv.wait(dv, dv.n)
            dv.do(lambda e, c=c: e.tensor_scalar(
                combT[:, c * 16:(c + 1) * 16], msk, gsum[:, 0:1], None, ALU.mult))

        phase8_pieces = []
        for g in range(0, n_ag // 2):
            phase8_pieces.append((lambda g=g: emit_ln2(g), g))
            phase8_pieces.append((lambda g=g: emit_gate(2 * g), g))
            phase8_pieces.append((lambda g=g: emit_gate(2 * g + 1), g))

        YB2 = W1[:, 0:4096].rearrange("p (c t) -> p c t", c=8)
        pe.wait(ql_wo2)
        pe.wait(dv, m_ctx_done)
        op_evac = {}
        ln3_ac = {}

        def emit_outproj(g):
            yo = (g % 2) * 256
            if (g - 2) in ln3_ac:
                dv.wait(ac, ln3_ac[g - 2])
            for m in range(KCH):
                bank = pb[m % 2]
                prev = op_evac.get((g, m - 2)) or op_evac.get((g - 1, m + 6))
                if prev is not None:
                    pe.wait(dv, prev)
                for k in range(KCH):
                    pe.do(lambda e, k=k, m=m, g=g, bank=bank: e.matmul(
                        bank[:, 0:256], WO[:, k, 128 * m:128 * m + 128],
                        AQ[:, k, g * 256:(g + 1) * 256],
                        start=(k == 0), stop=(k == KCH - 1)),
                        inc=(k == KCH - 1))
                m_mm = pe.n
                dv.wait(pe, m_mm)
                dv.do(lambda e, m=m, bank=bank, yo=yo: e.tensor_scalar(
                    YB2[:, m, yo:yo + 256], bank[:, 0:256], bo_t[:, m:m + 1],
                    None, ALU.add))
                op_evac[(g, m)] = dv.n
            return dv.n

        def emit_ln3(g):
            yo = (g % 2) * 256
            m_y = op_evac[(g, KCH - 1)]
            emit_ln(lambda k, yo=yo: YB2[:, k, yo:yo + 256], cast_needed=False,
                    apply=apply_residual(
                        lambda k, yo=yo: YB2[:, k, yo:yo + 256],
                        lambda k, g=g: AR[:, k, g * 256:(g + 1) * 256]),
                    extra_waits=lambda m_y=m_y: (pe.wait(dv, m_y),
                                                 ac.wait(dv, m_y)), W=256)
            ln3_ac[g] = ac.n

        for g in range(n_ag + 1):
            if g < n_ag:
                emit_outproj(g)
            if g >= 1:
                emit_ln3(g - 1)
                while phase8_pieces and phase8_pieces[0][1] <= g - 1:
                    phase8_pieces.pop(0)[0]()
        while phase8_pieces:
            phase8_pieces.pop(0)[0]()
        m_x1 = dv.n

        m_c_half = {}
        m_c_half[0] = dv.n
        pieces = []
        g0h1, g1h1 = n_ag // 2, n_ag
        for g in range(g0h1, g1h1):
            pieces.append(lambda g=g: emit_ln2(g))
            pieces.append(lambda g=g: emit_gate(2 * g))
            pieces.append(lambda g=g: emit_gate(2 * g + 1))
        m_xn2 = dv.n

        ACCF = AQ[:, :, :].bitcast(F32)
        HEV = AC[:, 0:4 * FH].rearrange("p (c t) -> p c t", c=4)
        WVA = [W1[:, :].rearrange("p (c f) -> p c f", c=16),
               W2[:, :].rearrange("p (c f) -> p c f", c=16)]
        WOB = [W3[:, 0:4096].rearrange("p (c f) -> p c f", c=4),
               W4[:, 0:4096].rearrange("p (c f) -> p c f", c=4)]
        SIL = AK[:, 0:1024].bitcast(F32)[:, 0:512]
        TMP = AK[:, 1024:2048].bitcast(F32)[:, 0:512]

        def win_ap(e_):
            if e_ < 16:
                return ewin[e_].rearrange("(c p) h -> p c h", p=128)
            return wsin[:, (e_ - 16) * 512:(e_ - 15) * 512]\
                .rearrange("(c p) h -> p c h", p=128)

        def vin_ap(e_):
            if e_ < 16:
                return evin[e_].rearrange("(c p) h -> p c h", p=128)
            return wvin[:, (e_ - 16) * 512:(e_ - 15) * 512]\
                .rearrange("(c p) h -> p c h", p=128)

        def wout_ap(e_):
            if e_ < 16:
                return ewout[e_].rearrange("(c p) d -> p c d", p=128)
            return wsout[(e_ - 16) * 512:(e_ - 15) * 512, :]\
                .rearrange("(c p) d -> p c d", p=128)

        wv_marks, wo_marks, a_done, b_done, badd_last = {}, {}, {}, {}, {}

        def load_wv(idx, e_):
            bi = idx % 2
            if (idx - 2) in a_done:
                ql_wv[bi].wait(pe, a_done[idx - 2])
            ql_wv[bi].do(lambda e, e_=e_, bi=bi: e.dma_start(
                out=WVA[bi][:, 0:8, :], in_=win_ap(e_)))
            ql_wv[bi].do(lambda e, e_=e_, bi=bi: e.dma_start(
                out=WVA[bi][:, 8:16, :], in_=vin_ap(e_)))
            wv_marks[idx] = (ql_wv[bi], ql_wv[bi].n)

        def load_wo(idx, e_):
            bi = idx % 2
            if (idx - 2) in b_done:
                ql_wb[bi].wait(pe, b_done[idx - 2])
            ql_wb[bi].do(lambda e, e_=e_, bi=bi: e.dma_start(
                out=WOB[bi][:, :, :], in_=wout_ap(e_)))
            wo_marks[idx] = (ql_wb[bi], ql_wb[bi].n)

        for qq in (ql_wv[0], ql_wv[1], ql_wb[0], ql_wb[1]):
            qq.wait(dv, m_x1)
            qq.wait(dv, m_ctx_done)
            qq.wait(ac, ac.n)

        def emit_ln4_store(half):
            m_acc = dv.n
            WL = 256 if half == 0 else 512
            xa = xbt[:, 0:512] if WL == 256 else AC[:, 0:1024]
            xs = xbt[:, 512:1024] if WL == 256 else AC[:, 1024:2048]
            for gg in range(FH // WL):
                goff = half * FH + gg * WL
                loff = gg * WL
                emit_ln(lambda k, loff=loff, WL=WL: ACCF[:, k, loff:loff + WL],
                        cast_needed=True,
                        apply=apply_residual(
                            lambda k, loff=loff, WL=WL: ACCF[:, k, loff:loff + WL],
                            lambda k, goff=goff, WL=WL: AR[:, k, goff:goff + WL],
                            W=WL),
                        extra_waits=lambda m_acc=m_acc: pe.wait(dv, m_acc),
                        W=WL, xa=xa, xs=xs)
                m_g_done = dv.n
                h2.wait(dv, m_g_done)
                h2.do(lambda e, half=half, goff=goff, WL=WL: e.dma_start(
                    out=outT[:, goff:goff + WL]
                    .rearrange("(c p) t -> p c t", p=128),
                    in_=AR[:, :, goff:goff + WL]))

        seq = [(half, e_) for half in range(2) for e_ in range(NEXP)]
        load_wv(0, seq[0][1])
        load_wo(0, seq[0][1])
        load_wv(1, seq[1][1])
        load_wo(1, seq[1][1])

        for idx, (half, e_) in enumerate(seq):
            bi = idx % 2
            gated = e_ < 16
            hbase = half * FH
            pe.wait(*wv_marks[idx])
            if idx == 0:
                pe.wait(dv, m_c_half[0])
            if idx == NEXP:
                pe.wait(dv, m_c_half[1])
            chain_marks = {}
            for sub in range(nsub):
                t0 = hbase + sub * FSUB
                ncw = FSUB // 128
                if gated:
                    cbb = pb[4 + sub % 2]
                    if (idx - 1) in b_done:
                        pe.wait(dv, badd_last[idx - 1])
                    else:
                        pe.wait(dv, dv.n)
                    for ci in range(ncw):
                        cgl = (t0 // 128) + ci
                        pe.do(lambda e, e_=e_, cgl=cgl, ci=ci, cbb=cbb: e.matmul(
                            cbb[0:1, ci * 128:ci * 128 + 128],
                            combT[:, cgl * 16 + e_:cgl * 16 + e_ + 1],
                            identb[:, :], start=True, stop=True))
                    m_sel = pe.n
                    csl = crow[0:1, 0:FSUB]
                    dv.wait(pe, m_sel)
                    dv.do(lambda e, csl=csl, cbb=cbb, w=FSUB: e.tensor_copy(
                        csl, cbb[0:1, 0:w]))
                    m_crow = dv.n
                    pe.wait(dv, m_crow)
                    pe.do(lambda e, csl=csl, cbb=cbb, w=FSUB: e.matmul(
                        cbb[:, 0:w], ones[0:1, :], csl, start=True, stop=True))
                for hc in range(HIDC):
                    hb = pb[hc % 2]
                    vb = pb[2 + hc % 2]
                    prevc = chain_marks.get((sub, hc - 2))
                    if prevc is not None:
                        pe.wait(dv, prevc[0])
                        pe.wait(ac, prevc[1])
                    for k in range(KCH):
                        pe.do(lambda e, k=k, hc=hc, t0=t0, bi=bi, hb=hb, w=FSUB:
                              e.matmul(hb[:, 0:w],
                                       WVA[bi][:, k, 128 * hc:128 * hc + 128],
                                       AX[:, k, t0:t0 + w],
                                       start=(k == 0), stop=(k == KCH - 1)),
                              inc=(k == KCH - 1))
                    m_hin = pe.n
                    for k in range(KCH):
                        pe.do(lambda e, k=k, hc=hc, t0=t0, bi=bi, vb=vb, w=FSUB:
                              e.matmul(vb[:, 0:w],
                                       WVA[bi][:, 8 + k, 128 * hc:128 * hc + 128],
                                       AX[:, k, t0:t0 + w],
                                       start=(k == 0), stop=(k == KCH - 1)),
                              inc=(k == KCH - 1))
                    m_vin = pe.n
                    ac.wait(pe, m_hin)
                    prevt = chain_marks.get((sub, hc - 1))
                    if prevt is not None:
                        ac.wait(dv, prevt[0])
                    ac.do(lambda e, hb=hb, w=FSUB: e.activation(
                        SIL[:, 0:w], hb[:, 0:w], AF.Silu))
                    m_sil = ac.n
                    dv.wait(ac, m_sil)
                    dv.wait(pe, m_vin)
                    ht = HEV[:, hc, sub * FSUB:sub * FSUB + FSUB]
                    if gated:
                        dv.do(lambda e, vb=vb, w=FSUB: e.tensor_tensor(
                            TMP[:, 0:w], SIL[:, 0:w], vb[:, 0:w], ALU.mult))
                        mt = dv.n
                        dv.wait(dv, mt)
                        cbb = pb[4 + sub % 2]
                        dv.do(lambda e, ht=ht, cbb=cbb, w=FSUB: e.tensor_tensor(
                            ht, TMP[:, 0:w], cbb[:, 0:w], ALU.mult))
                    else:
                        dv.do(lambda e, ht=ht, vb=vb, w=FSUB: e.tensor_tensor(
                            ht, SIL[:, 0:w], vb[:, 0:w], ALU.mult))
                    chain_marks[(sub, hc)] = (dv.n, ac.n)
            a_done[idx] = pe.n
            m_h = dv.n
            if half == 0 and pieces:
                pieces.pop(0)()
                if e_ == 0 and pieces:
                    pieces.pop(0)()
                if not pieces:
                    m_c_half[1] = dv.n
            if half == 1 and e_ == 0:
                emit_ln4_store(0)
            pe.wait(*wo_marks[idx])
            pe.wait(dv, m_h)
            badd = {}
            if e_ == 0:
                pe.wait(ac, ac.n)
            for sub in range(nsub):
                for m in range(KCH):
                    mi = m + sub * KCH
                    bb = pb[4 + mi % 2]
                    prevb = badd.get(mi - 2)
                    if prevb is not None:
                        pe.wait(dv, prevb)
                    for k in range(HIDC):
                        pe.do(lambda e, k=k, m=m, sub=sub, bi=bi, bb=bb, w=FSUB:
                              e.matmul(bb[:, 0:w],
                                       WOB[bi][:, k, 128 * m:128 * m + 128],
                                       HEV[:, k, sub * FSUB:sub * FSUB + w],
                                       start=(k == 0), stop=(k == HIDC - 1)),
                              inc=(k == HIDC - 1))
                    m_b = pe.n
                    dv.wait(pe, m_b)
                    accsl = ACCF[:, m, sub * FSUB:(sub + 1) * FSUB]
                    if e_ == 0:
                        dv.do(lambda e, bb=bb, accsl=accsl, w=FSUB:
                              e.tensor_copy(accsl, bb[:, 0:w]))
                    else:
                        dv.do(lambda e, bb=bb, accsl=accsl, w=FSUB:
                              e.tensor_tensor(accsl, accsl, bb[:, 0:w], ALU.add))
                    badd[mi] = dv.n
            badd_last[idx] = dv.n
            b_done[idx] = pe.n
            if idx + 2 < len(seq):
                load_wv(idx + 2, seq[idx + 2][1])
                load_wo(idx + 2, seq[idx + 2][1])

        emit_ln4_store(1)
        m_store = h2.n
        prog.add("sync", lambda e: e.wait_ge(sems["sH2"], m_store))

        with nc.Block() as blk:
            @blk.sync
            def _(e):
                for fn in prog.t["sync"]:
                    fn(e)

            @blk.gpsimd
            def _(e):
                for fn in prog.t["gpsimd"]:
                    fn(e)

            @blk.tensor
            def _(e):
                for fn in prog.t["tensor"]:
                    fn(e)

            @blk.scalar
            def _(e):
                for fn in prog.t["scalar"]:
                    fn(e)

            @blk.vector
            def _(e):
                for fn in prog.t["vector"]:
                    fn(e)

    return nc


def make_in_maps(cfg, A):
    NO = cfg["NO"]
    NF = 3 * NO
    S, FS = NO * 128, NF * 128
    L = 8 * S
    x = np.ascontiguousarray(A["x"].reshape(L, DM).astype(np.float32))
    xT = np.ascontiguousarray(x.T)
    pos = np.arange(L, dtype=np.float64)
    cw = np.cos(np.pi / 2 * pos / L).astype(np.float32)
    sw = np.sin(np.pi / 2 * pos / L).astype(np.float32)

    b_qkv = A["b_qkv"].astype(np.float32)
    com = dict(
        wqkv=np.ascontiguousarray(A["W_qkv"].astype(np.float32)),
        wout=np.ascontiguousarray(A["W_out"].astype(np.float32)),
        gwt=np.ascontiguousarray(
            A["gate_w"].astype(np.float32).T.reshape(8, 128, 16)
            .transpose(1, 0, 2).reshape(128, 128)),
        wsin=np.ascontiguousarray(A["ws_in"].astype(np.float32)),
        wvin=np.ascontiguousarray(A["vs_in"].astype(np.float32)),
        wsout=np.ascontiguousarray(A["ws_out"].astype(np.float32)),
        ewin=np.ascontiguousarray(A["e_win"].astype(np.float32)),
        evin=np.ascontiguousarray(A["e_vin"].astype(np.float32)),
        ewout=np.ascontiguousarray(A["e_wout"].astype(np.float32)),
        bqt=np.ascontiguousarray(b_qkv[0:1024].reshape(8, 128).T),
        bkv=np.ascontiguousarray(b_qkv[1024:3072].reshape(1, 2048)),
        bot=np.ascontiguousarray(A["b_out"].astype(np.float32).reshape(8, 128).T),
        identin=np.eye(128, dtype=np.float32),
    )
    maps = []
    for c in range(8):
        own = np.arange(c * S, (c + 1) * S)
        pair = c // 2
        fidx = np.concatenate([np.arange(0, pair * 2 * S),
                               np.arange((pair + 1) * 2 * S, L)])
        fidx = fidx[:FS] if c % 2 == 0 else fidx[FS:]
        chunk_pos = np.concatenate([own, fidx])
        m = dict(com)
        m["xTo"] = np.ascontiguousarray(xT[:, own])
        m["xTf"] = np.ascontiguousarray(xT[:, fidx])
        m["cosbo"] = np.ascontiguousarray(np.broadcast_to(cw[own], (128, S)))
        m["sinbo"] = np.ascontiguousarray(np.broadcast_to(sw[own], (128, S)))
        m["cost"] = np.ascontiguousarray(cw[chunk_pos].reshape(-1, 128).T)
        m["sint"] = np.ascontiguousarray(sw[chunk_pos].reshape(-1, 128).T)
        maps.append(m)
    return maps


def gather_out(cfg, results):
    NO = cfg["NO"]
    S = NO * 128
    L = 8 * S
    outs = [np.asarray(r["outT"], dtype=np.float32) for r in results]
    full = np.concatenate(outs, axis=1)
    return np.ascontiguousarray(full.T).reshape(1, L, DM)


def kernel(L0_attn_x, L0_attn_W_qkv, L0_attn_b_qkv, L0_attn_W_out, L0_attn_b_out,
           L0_ffn_gate_w, L0_ffn_ws_in, L0_ffn_vs_in, L0_ffn_ws_out,
           L0_ffn_e_win, L0_ffn_e_vin, L0_ffn_e_wout):
    cfg = REAL
    A = dict(x=np.asarray(L0_attn_x), W_qkv=np.asarray(L0_attn_W_qkv),
             b_qkv=np.asarray(L0_attn_b_qkv), W_out=np.asarray(L0_attn_W_out),
             b_out=np.asarray(L0_attn_b_out), gate_w=np.asarray(L0_ffn_gate_w),
             ws_in=np.asarray(L0_ffn_ws_in), vs_in=np.asarray(L0_ffn_vs_in),
             ws_out=np.asarray(L0_ffn_ws_out), e_win=np.asarray(L0_ffn_e_win),
             e_vin=np.asarray(L0_ffn_e_vin), e_wout=np.asarray(L0_ffn_e_wout))
    nc = build(cfg)
    in_maps = make_in_maps(cfg, A)
    res = run_bass_kernel_spmd(nc, in_maps, core_ids=list(range(8)))
    return gather_out(cfg, res.results)


# version 11
# speedup vs baseline: 1.0296x; 1.0151x over previous
import numpy as np
import ml_dtypes
import concourse.bass as bass
import concourse.mybir as mybir
from concourse.bass_utils import run_bass_kernel_spmd

F32 = mybir.dt.float32
BF16 = mybir.dt.bfloat16
I32 = mybir.dt.int32
AF = mybir.ActivationFunctionType
ALU = mybir.AluOpType
X = mybir.AxisListType.X
ds = bass.ds

MAGIC = 77001.0
MAGIC_I = int(np.float32(MAGIC).view(np.int32))
POLL_LIMIT = 30000

H, DM, KCH = 16, 1024, 8
NEXP, HIDC = 18, 4
EPS_LN, EPS_N = 1e-5, 1e-4
G = 256

REAL = dict(NO=16)


class Prog:
    ENG = ("sync", "gpsimd", "tensor", "scalar", "vector")

    def __init__(self):
        self.t = {e: [] for e in self.ENG}

    def add(self, eng, fn):
        self.t[eng].append(fn)


class Q:
    def __init__(self, prog, eng, sem, step=1):
        self.p, self.eng, self.sem, self.step, self.n = prog, eng, sem, step, 0

    def do(self, fn, inc=True):
        if inc:
            self.n += self.step
            sem, st = self.sem, self.step
            self.p.add(self.eng, lambda e, fn=fn, sem=sem, st=st: fn(e).then_inc(sem, st))
        else:
            self.p.add(self.eng, lambda e, fn=fn: fn(e))
        return self.n

    def wait(self, q2, val=None):
        v = q2.n if val is None else val
        if v <= 0:
            return
        sem = q2.sem
        self.p.add(self.eng, lambda e, sem=sem, v=v: e.wait_ge(sem, v))


def build(cfg, exchange="poll", detect_races=True):
    NO = cfg["NO"]
    NF = 3 * NO
    S = NO * 128
    FS = NF * 128
    NC = NO + NF
    n_og, n_fg = NO // 2, NF // 2
    n_ag = S // G
    FH = S // 2
    FSUB = min(512, FH)
    nsub = FH // FSUB

    nc = bass.Bass("TRN2", target_bir_lowering=False,
                   detect_race_conditions=detect_races)

    xTo = nc.dram_tensor("xTo", [DM, S], F32, kind="ExternalInput")
    xTf = nc.dram_tensor("xTf", [DM, FS], F32, kind="ExternalInput")
    wqkv = nc.dram_tensor("wqkv", [DM, 3 * DM], F32, kind="ExternalInput")
    wout = nc.dram_tensor("wout", [DM, DM], F32, kind="ExternalInput")
    gwt = nc.dram_tensor("gwt", [128, 128], F32, kind="ExternalInput")
    wsin = nc.dram_tensor("wsin", [DM, DM], F32, kind="ExternalInput")
    wvin = nc.dram_tensor("wvin", [DM, DM], F32, kind="ExternalInput")
    wsout = nc.dram_tensor("wsout", [DM, DM], F32, kind="ExternalInput")
    ewin = nc.dram_tensor("ewin", [16, DM, 512], F32, kind="ExternalInput")
    evin = nc.dram_tensor("evin", [16, DM, 512], F32, kind="ExternalInput")
    ewout = nc.dram_tensor("ewout", [16, 512, DM], F32, kind="ExternalInput")
    bqt = nc.dram_tensor("bqt", [128, 8], F32, kind="ExternalInput")
    bkv = nc.dram_tensor("bkv", [1, 2048], F32, kind="ExternalInput")
    bot = nc.dram_tensor("bot", [128, 8], F32, kind="ExternalInput")
    cosbo = nc.dram_tensor("cosbo", [128, S], F32, kind="ExternalInput")
    sinbo = nc.dram_tensor("sinbo", [128, S], F32, kind="ExternalInput")
    cost = nc.dram_tensor("cost", [128, NC], F32, kind="ExternalInput")
    sint = nc.dram_tensor("sint", [128, NC], F32, kind="ExternalInput")
    identin = nc.dram_tensor("identin", [128, 128], F32, kind="ExternalInput")
    outT = nc.dram_tensor("outT", [DM, S], F32, kind="ExternalOutput")
    sdata = nc.dram_tensor("sdata", [128, 2080], F32, kind="Internal", addr_space="Shared")
    sflag = nc.dram_tensor("sflag", [1, 16], F32, kind="Internal", addr_space="Shared")
    cc_in = nc.dram_tensor("cc_in", [128, 1040], F32, kind="Internal")
    cc_out = nc.dram_tensor("cc_out", [128, 1040], F32, kind="Internal")

    from contextlib import ExitStack
    with ExitStack() as ctx:
        sb = lambda n, s, d: ctx.enter_context(nc.sbuf_tensor(n, s, d))
        AR = sb("AR", [128, KCH, S], F32)
        AX = sb("AX", [128, KCH, S], BF16)
        AQ = sb("AQ", [128, KCH, S], BF16)
        AW = sb("AW", [128, 24576], BF16)
        AC = sb("AC", [128, 4096], BF16)
        AK = sb("AK", [128, 3136], BF16)
        AE = sb("AE", [128, 2080], F32)
        RBF = sb("RBF", [1, 1024], BF16)
        consts = sb("consts", [128, 8], F32)
        bq_t = sb("bq_t", [128, 8], F32)
        bo_t = sb("bo_t", [128, 8], F32)
        gwTb = sb("gwTb", [128, 128], BF16)
        cosTt = sb("cosTt", [128, NC], F32)
        sinTt = sb("sinTt", [128, NC], F32)
        identb = sb("identb", [128, 128], BF16)
        combT = sb("combT", [128, 16 * NO], BF16)
        crow = sb("crow", [1, 512], BF16)
        ones = sb("ones", [128, 128], BF16)
        xbt = sb("xbt", [128, 1024], BF16)
        LNB = sb("LNB", [128, 1024], BF16)
        LNT = sb("LNT", [128, 256], BF16)
        magic = sb("magic", [1, 8], F32)
        flg = sb("flg", [1, 16], F32)
        pb = [ctx.enter_context(nc.psum_tensor(f"pb{i}", [128, 512], F32))
              for i in range(8)]
        sems = {}
        for sname in ("sXL", "sTB", "sWKV", "sCF", "sXF0", "sXF1", "sWQ",
                      "sWO2", "sWV0", "sWV1", "sWB0", "sWB1", "sZF", "sXW",
                      "sFW", "sPL", "sPR", "sH2", "sPE", "sDVE", "sACT", "sSX"):
            sems[sname] = ctx.enter_context(nc.semaphore(sname))

        prog = Prog()
        ql_x = Q(prog, "sync", sems["sXL"], 16)
        ql_tb = Q(prog, "sync", sems["sTB"], 16)
        ql_wkv = Q(prog, "gpsimd", sems["sWKV"], 16)
        ql_cf = Q(prog, "gpsimd", sems["sCF"], 16)
        ql_xf = [Q(prog, "gpsimd", sems["sXF0"], 16),
                 Q(prog, "gpsimd", sems["sXF1"], 16)]
        ql_wq = Q(prog, "gpsimd", sems["sWQ"], 16)
        ql_wo2 = Q(prog, "gpsimd", sems["sWO2"], 16)
        ql_wv = [Q(prog, "gpsimd", sems["sWV0"], 16),
                 Q(prog, "gpsimd", sems["sWV1"], 16)]
        ql_wb = [Q(prog, "gpsimd", sems["sWB0"], 16),
                 Q(prog, "gpsimd", sems["sWB1"], 16)]
        ql_zf = Q(prog, "sync", sems["sZF"], 16)
        ql_xw = Q(prog, "sync", sems["sXW"], 16)
        ql_fw = Q(prog, "sync", sems["sFW"], 16)
        ql_pr = Q(prog, "sync", sems["sPR"], 16)
        h2 = Q(prog, "sync", sems["sH2"], 16)
        pe = Q(prog, "tensor", sems["sPE"])
        dv = Q(prog, "vector", sems["sDVE"])
        ac = Q(prog, "scalar", sems["sACT"])
        sx = Q(prog, "sync", sems["sSX"])

        W1 = AW[:, 0:8192]

        W2 = AW[:, 8192:16384]
        W3 = AW[:, 16384:20480]
        W4 = AW[:, 20480:24576]
        kcs2 = AK[:, 0:2048].rearrange("p (h x) -> p h x", h=16)
        vt2 = AK[:, 2048:3088].rearrange("p (h x) -> p h x", h=16)
        TMPA = AK[:, 0:1024].bitcast(F32)[:, 0:512]
        TMPB = AK[:, 1024:2048].bitcast(F32)[:, 0:512]

        bkv_r = AW[0:1, 20480:22528]
        RSTD_B = None
        CEPS = consts[:, 0:1]
        CNH = consts[:, 1:2]
        CM1 = consts[:, 3:4]

        ql_x.do(lambda e: e.dma_start(
            out=AR[:, :, :], in_=xTo[:, :].rearrange("(c p) t -> p c t", p=128)))
        for t, src in ((bq_t, bqt), (bo_t, bot), (cosTt, cost), (sinTt, sint)):
            ql_tb.do(lambda e, t=t, src=src: e.dma_start(out=t[:, :], in_=src[:, :]))

        ql_wkv.do(lambda e: e.dma_start(
            out=AW[:, 0:16384].rearrange("p (c f) -> p c f", c=8),
            in_=wqkv[:, 1024:3072].rearrange("(c p) f -> p c f", p=128)))
        ql_cf.do(lambda e: e.dma_start(out=W3[:, 0:S], in_=cosbo[:, :]))
        ql_cf.do(lambda e: e.dma_start(out=W3[:, 2048:2048 + S], in_=sinbo[:, :]))
        ql_cf.do(lambda e: e.dma_start(out=gwTb[:, :], in_=gwt[:, :]))
        ql_cf.do(lambda e: e.dma_start(out=bkv_r[:, :], in_=bkv[:, :]))
        ql_cf.do(lambda e: e.dma_start(out=identb[:, :], in_=identin[:, :]))

        dv.do(lambda e: e.memset(ones[:, :], 1.0))
        dv.do(lambda e: e.memset(consts[:, 0:1], EPS_LN))
        dv.do(lambda e: e.memset(consts[:, 1:2], -0.5))
        dv.do(lambda e: e.memset(consts[:, 2:3], EPS_N))
        dv.do(lambda e: e.memset(consts[:, 3:4], -1.0))
        dv.do(lambda e: e.memset(vt2[:, :, 64:65], 1.0))
        dv.do(lambda e: e.memset(magic[0:1, 0:1], MAGIC))
        dv.do(lambda e: e.memset(magic[0:1, 1:2], 0.0))
        m_consts = dv.n

        ql_zf.wait(dv, m_consts)

        def _zeroflag(e):
            slot = e.partition_id() % 2
            e.dma_start(out=sflag[0:1, ds(slot, 1)], in_=magic[0:1, 1:2])\
                .then_inc(sems["sZF"], 16)
        ql_zf.n += 16
        prog.add("sync", _zeroflag)

        WKV = AW[:, 0:16384].rearrange("p (c f) -> p c f", c=8)
        COSB = W3[:, 0:2048]
        SINB = W3[:, 2048:4096]

        ROW_A = AE[0:1, 0:512]
        ROW_B = AE[0:1, 512:1024]
        RB_STD = RBF[0:1, 0:512]
        RB_NMR = RBF[0:1, 512:1024]
        RSTD_B = LNB[:, 0:512]
        NMR_B = LNB[:, 512:1024]

        def emit_ln(src, cast_needed, apply, extra_waits=None, W=256,
                    xa=None, xs=None):
            pe.wait(dv, emit_ln.last_lnb)
            pe.wait(ac, emit_ln.last_rstd)
            if extra_waits:
                extra_waits()
            m_sq = dict(emit_ln.prev_msq)
            for k in range(KCH):
                pp = k % 2
                if cast_needed:
                    xb = xa[:, pp * W:(pp + 1) * W]
                    sq = xs[:, pp * W:(pp + 1) * W]
                else:
                    sq = xbt[:, pp * 512:pp * 512 + W]
                if cast_needed:
                    if (k - 2) in m_sq:
                        dv.wait(pe, m_sq[k - 2][1])
                    dv.do(lambda e, xb=xb, k=k: e.tensor_copy(xb, src(k)))
                    m_cast = dv.n
                    ac.wait(dv, m_cast)
                    if (k - 2) in m_sq:
                        ac.wait(pe, m_sq[k - 2][1])
                    ac.do(lambda e, sq=sq, xb=xb: e.activation(sq, xb, AF.Square))
                    msq_a = ac.n
                    pe.wait(dv, m_cast)
                    xin = xb
                else:
                    if (k - 2) in m_sq:
                        ac.wait(pe, m_sq[k - 2][1])
                    ac.do(lambda e, sq=sq, k=k: e.activation(sq, src(k), AF.Square))
                    msq_a = ac.n
                    xin = None
                pe.wait(ac, msq_a)
                st, sp = (k == 0), (k == KCH - 1)
                if xin is not None:
                    pe.do(lambda e, xin=xin, st=st, sp=sp: e.matmul(
                        pb[6][0:1, 0:W], ones[:, 0:1], xin, start=st, stop=sp))
                else:
                    pe.do(lambda e, k=k, st=st, sp=sp: e.matmul(
                        pb[6][0:1, 0:W], ones[:, 0:1], src(k), start=st, stop=sp))
                pe.do(lambda e, sq=sq, st=st, sp=sp: e.matmul(
                    pb[7][0:1, 0:W], ones[:, 0:1], sq, start=st, stop=sp))
                m_sq[k] = (pe.n - 1, pe.n)
            m_stats = pe.n
            dv.wait(pe, m_stats)
            dv.do(lambda e: e.tensor_scalar(ROW_A[:, 0:W], pb[6][0:1, 0:W],
                                            1.0 / DM, None, ALU.mult))
            dv.do(lambda e: e.tensor_scalar(ROW_B[:, 0:W], pb[7][0:1, 0:W],
                                            1.0 / DM, None, ALU.mult))
            m_mq = dv.n
            dv.wait(dv, m_mq)
            dv.do(lambda e: e.tensor_tensor(pb[6][0:1, 0:W], ROW_A[:, 0:W],
                                            ROW_A[:, 0:W], ALU.mult))
            m_msq = dv.n
            dv.wait(dv, m_msq)
            dv.do(lambda e: e.tensor_tensor(ROW_B[:, 0:W], ROW_B[:, 0:W],
                                            pb[6][0:1, 0:W], ALU.subtract))
            m_var = dv.n
            ac.wait(dv, m_var)
            ac.do(lambda e: e.activation(pb[7][0:1, 0:W], ROW_B[:, 0:W],
                                         AF.Ln, bias=CEPS[0:1, :]))
            m_l = ac.n
            ac.wait(ac, m_l)
            ac.do(lambda e: e.activation(RB_STD[0:1, 0:W], pb[7][0:1, 0:W],
                                         AF.Exp, scale=CNH[0:1, :]))
            m_rstd = ac.n
            emit_ln.last_rstd = m_rstd
            dv.wait(ac, m_rstd)
            dv.do(lambda e: e.tensor_tensor(RB_NMR[0:1, 0:W], ROW_A[:, 0:W],
                                            RB_STD[0:1, 0:W], ALU.mult))
            m_rows = dv.n
            pe.wait(dv, m_rows)
            pe.do(lambda e: e.matmul(pb[6][:, 0:W], ones[0:1, :],
                                     RB_STD[0:1, 0:W], start=True, stop=True))
            pe.do(lambda e: e.matmul(pb[7][:, 0:W], ones[0:1, :],
                                     RB_NMR[0:1, 0:W], start=True, stop=True))
            m_bcast = pe.n
            dv.wait(pe, m_bcast)
            dv.do(lambda e: e.tensor_copy(LNB[:, 0:W], pb[6][:, 0:W]))
            dv.do(lambda e: e.tensor_copy(LNB[:, 512:512 + W], pb[7][:, 0:W]))
            m_lnb = dv.n
            emit_ln.last_lnb = m_lnb
            emit_ln.prev_msq = {k - KCH: v for k, v in m_sq.items() if k >= KCH - 2}
            apply()
            return dv.n
        emit_ln.last_lnb = 0
        emit_ln.last_rstd = 0
        emit_ln.prev_msq = {}

        def apply_to_bf16(dst_of, src_of, W=256):
            def _a():
                for k in range(KCH):
                    for j in range(W // 256):
                        o = j * 256
                        dv.wait(dv, dv.n)
                        dv.do(lambda e, k=k, o=o: e.tensor_tensor(
                            LNT[:, 0:256], src_of(k)[:, o:o + 256],
                            RSTD_B[:, o:o + 256], ALU.mult))
                        mm = dv.n
                        dv.wait(dv, mm)
                        dv.do(lambda e, k=k, o=o: e.tensor_tensor(
                            dst_of(k)[:, o:o + 256], LNT[:, 0:256],
                            NMR_B[:, o:o + 256], ALU.subtract))
            return _a

        def apply_residual(src_of, res_of, W=256):
            def _a():
                for k in range(KCH):
                    for j in range(W // 256):
                        o = j * 256
                        dv.wait(dv, dv.n)
                        dv.do(lambda e, k=k, o=o: e.tensor_tensor(
                            LNT[:, 0:256], src_of(k)[:, o:o + 256],
                            RSTD_B[:, o:o + 256], ALU.mult))
                        mm = dv.n
                        dv.wait(dv, mm)
                        dv.do(lambda e, k=k, o=o: e.tensor_tensor(
                            LNT[:, 0:256], LNT[:, 0:256],
                            NMR_B[:, o:o + 256], ALU.subtract))
                        mm2 = dv.n
                        dv.wait(dv, mm2)
                        dv.do(lambda e, k=k, o=o: e.tensor_tensor(
                            res_of(k)[:, o:o + 256], res_of(k)[:, o:o + 256],
                            LNT[:, 0:256], ALU.add))
            return _a

        XFP = [AQ[:, :, b * 512:(b + 1) * 512] for b in range(min(4, S // 512))]
        NBUF = len(XFP)
        assert NBUF >= 2, "need at least 1024 own tokens per core"
        n_og4 = NO // 4
        n_fg4 = (3 * NO) // 4
        NKVG = n_og4 + n_fg4

        xf_marks = {}
        for fgi in range(min(2, n_fg4)):
            ql_xf[fgi % 2].do(lambda e, fgi=fgi: e.dma_start(
                out=XFP[fgi % NBUF][:, :, :],
                in_=xTf[:, fgi * 512:(fgi + 1) * 512]
                .rearrange("(c p) t -> p c t", p=128)))
            xf_marks[fgi] = (ql_xf[fgi % 2], ql_xf[fgi % 2].n)

        kv_bank_marks = {}
        pe.wait(ql_x)
        pe.wait(ql_tb)
        pe.wait(dv, m_consts)
        pe.wait(ql_wkv)
        pe.wait(ql_cf)
        ac.wait(dv, m_consts)
        ac.wait(ql_tb)
        dv.wait(ql_tb)
        dv.wait(ql_x)

        kvproj_done = {}

        def emit_kvacc(c, m_ev_d, m_ev_a, heads=range(H)):
            pe.wait(dv, m_ev_d)
            pe.wait(ac, m_ev_a)
            for h in heads:
                bsel = 0 if h < 7 else (1 if h < 14 else 2)
                accb = pb[2 + bsel]
                col = 65 * (h - (0, 7, 14)[bsel])
                fc = (c == 0) and (h in (0, 7, 14))
                lc = (c == NC - 1) and (h in (6, 13, 15))
                pe.do(lambda e, h=h, accb=accb, col=col, fc=fc, lc=lc:
                      e.matmul(accb[:, col:col + 65], kcs2[:, h, :],
                               vt2[:, h, 0:65],
                               start=fc, stop=lc), inc=(h == H - 1))
            emit_kvacc.last = pe.n
        emit_kvacc.last = 0
        emit_kvacc.pending = []

        def ln_for_group(g):
            own = g < n_og4
            fg = g - n_og4
            if own:
                src = lambda k, g=g: AR[:, k, g * 512:(g + 1) * 512]
                extra = None
                apply = apply_to_bf16(
                    lambda k, g=g: AX[:, k, g * 512:(g + 1) * 512],
                    lambda k, g=g: AR[:, k, g * 512:(g + 1) * 512], W=512)
                xa, xs = AC[:, 0:1024], AC[:, 1024:2048]
            else:
                xfv = XFP[fg % NBUF]
                src = lambda k, xfv=xfv: xfv[:, k, 0:512]
                lq, lv = xf_marks[fg]
                extra = lambda lq=lq, lv=lv: (pe.wait(lq, lv), ac.wait(lq, lv),
                                              dv.wait(lq, lv))
                apply = apply_to_bf16(
                    lambda k, xfv=xfv: xfv[:, k, 0:512],
                    lambda k, xfv=xfv: xfv[:, k, 0:512], W=512)
                xa = xs = None
            return emit_ln(src, cast_needed=own, apply=apply, extra_waits=extra,
                           W=512, xa=xa, xs=xs)

        apply_marks = {0: ln_for_group(0)}
        for g in range(NKVG):
            own = g < n_og4
            fg = g - n_og4
            m_apply = apply_marks[g]

            if own:
                xn_of = lambda k, ci, g=g: AX[:, k, (g * 4 + ci) * 128:
                                              (g * 4 + ci) * 128 + 128]
            else:
                xn_of = lambda k, ci, xfv=XFP[fg % NBUF]: \
                    xfv[:, k, ci * 128:ci * 128 + 128]

            for ci in range(4):
                c = g * 4 + ci
                nmarks = {}
                for n in (0, 2, 1, 3):
                    bi3 = (ci * 4 + (0, 2, 1, 3).index(n)) % 3
                    bank = pb[(0, 1, 5)[bi3]]
                    prev = kv_bank_marks.get(bi3)
                    if prev is not None:
                        pe.wait(dv, prev[0])
                        pe.wait(ac, prev[1])
                    pe.wait(dv, m_apply)
                    for k in range(KCH):
                        pe.do(lambda e, k=k, ci=ci, n=n, bank=bank, xn_of=xn_of:
                              e.matmul(bank[:, 0:512], xn_of(k, ci),
                                       WKV[:, k, n * 512:(n + 1) * 512],
                                       start=(k == 0), stop=False), inc=False)
                    pe.do(lambda e, n=n, bank=bank: e.matmul(
                        bank[:, 0:512], ones[0:1, :],
                        bkv_r[0:1, n * 512:(n + 1) * 512],
                        start=False, stop=True))
                    m_proj = pe.n
                    if n < 2:
                        dv.wait(pe, m_proj)
                        dv.wait(pe, emit_kvacc.last)
                        bview = bank[:, 0:512].rearrange("p (h x) -> p h x", h=8)
                        dv.do(lambda e, bview=bview, n=n, c=c: e.tensor_scalar(
                            kcs2[:, 8 * n:8 * n + 8, 0:64], bview, 0.0,
                            cosTt[:, c:c + 1], ALU.max, ALU.mult))
                        dv.do(lambda e, bview=bview, n=n, c=c: e.tensor_scalar(
                            kcs2[:, 8 * n:8 * n + 8, 64:128], bview, 0.0,
                            sinTt[:, c:c + 1], ALU.max, ALU.mult))
                        kv_bank_marks[bi3] = (dv.n, ac.n)
                    else:
                        ac.wait(pe, m_proj)
                        ac.wait(pe, emit_kvacc.last)
                        bv2 = bank[:, 0:512].rearrange("p (h x) -> p h x", h=8)
                        ac.do(lambda e, bv2=bv2, n=n: e.copy(
                            vt2[:, 8 * (n - 2):8 * (n - 2) + 8, 0:64], bv2))
                        kv_bank_marks[bi3] = (dv.n, ac.n)
                    nmarks[n] = (dv.n, ac.n)
                    if n == 2:
                        emit_kvacc(c, nmarks[0][0], nmarks[2][1], range(0, 8))
                emit_kvacc(c, nmarks[1][0], nmarks[3][1], range(8, H))
            kvproj_done[g] = pe.n

            if g + 1 < NKVG:
                apply_marks[g + 1] = ln_for_group(g + 1)

            tgt = (g + 2) - n_og4
            if 2 <= tgt < n_fg4 and tgt not in xf_marks:
                qq = ql_xf[tgt % 2]
                qq.wait(pe, kvproj_done[g])
                qq.do(lambda e, tgt=tgt, b=tgt % NBUF: e.dma_start(
                    out=XFP[b][:, :, :],
                    in_=xTf[:, tgt * 512:(tgt + 1) * 512]
                    .rearrange("(c p) t -> p c t", p=128)))
                xf_marks[tgt] = (qq, qq.n)
        m_kv_mms = pe.n

        kvloc = AE[:, 0:1040]
        partner = AE[:, 1040:2080]
        dv.wait(pe, m_kv_mms)
        dv.do(lambda e: e.tensor_copy(kvloc[:, 0:455], pb[2][:, 0:455]))
        dv.do(lambda e: e.tensor_copy(kvloc[:, 455:910], pb[3][:, 0:455]))
        dv.do(lambda e: e.tensor_copy(kvloc[:, 910:1040], pb[4][:, 0:130]))
        m_kvloc = dv.n

        if exchange == "poll":
            ql_xw.wait(ql_zf)
            ql_xw.wait(dv, m_kvloc)

            def _exchange(e):
                off = (e.partition_id() % 2) * 1040
                e.dma_start(out=sdata[:, ds(off, 1040)], in_=kvloc)\
                    .then_inc(sems["sXW"], 16)
            ql_xw.n += 16
            prog.add("sync", _exchange)
            ql_fw.wait(ql_xw)

            def _flagwrite(e):
                slot = e.partition_id() % 2
                e.dma_start(out=sflag[0:1, ds(slot, 1)], in_=magic[0:1, 0:1])\
                    .then_inc(sems["sFW"], 16)
            ql_fw.n += 16
            prog.add("sync", _flagwrite)

            def _poll(e):
                e.wait_ge(sems["sFW"], 16)
                with e.register("more") as more, e.register("cnt") as cnt, \
                        e.register("it") as it, e.register("exp") as exp:
                    e.reg_mov(more, 1)
                    e.reg_mov(it, 0)
                    e.reg_mov(exp, 0)
                    with e.While(more):
                        e.reg_add(it, it, 1)
                        e.dma_start(out=flg[0:1, 0:2], in_=sflag[0:1, 0:2])\
                            .then_inc(sems["sPL"], 16)
                        e.reg_add(exp, exp, 16)
                        e.wait_ge(sems["sPL"], exp)
                        e.reg_mov(cnt, 0)
                        for i in range(2):
                            fv = e.value_load(flg[0:1, i:i + 1].bitcast(I32))
                            with e.register(f"eq{i}") as req:
                                e.reg_alu(req, fv, MAGIC_I, ALU.is_equal)
                                e.reg_alu(cnt, cnt, req, ALU.add)
                        e.reg_alu(more, cnt, 2, ALU.is_lt)
                        with e.register("lim") as lim:
                            e.reg_alu(lim, it, POLL_LIMIT, ALU.is_lt)
                            e.reg_alu(more, more, lim, ALU.bitwise_and)
                    poff = (1 - (e.partition_id() % 2)) * 1040
                    e.dma_start(out=partner, in_=sdata[:, ds(poff, 1040)])\
                        .then_inc(sems["sPR"], 16)
            prog.add("sync", _poll)
            ql_pr.n += 16
            sx.wait(ql_pr)
            prog.add("sync", lambda e: e.sem_inc(sems["sSX"], 1))
            sx.n += 1
            m_xdone = sx.n
        else:
            ql_xw.wait(dv, m_kvloc)
            ql_xw.do(lambda e: e.dma_start(out=cc_in[:, :], in_=kvloc))
            prog.add("gpsimd", lambda e: e.wait_ge(sems["sXW"], ql_xw.n))
            prog.add("gpsimd", lambda e: e.collective_compute(
                "AllReduce", ALU.add,
                replica_groups=[[0, 1], [2, 3], [4, 5], [6, 7]],
                ins=[cc_in[:, :].opt()],
                outs=[cc_out[:, :].opt()]).then_inc(sems["sPR"], 16))
            ql_pr.n += 16
            prog.add("sync", lambda e, v=ql_pr.n: e.wait_ge(sems["sPR"], v))
            ql_fw.do(lambda e: e.dma_start(out=kvloc, in_=cc_out[:, :]))
            dv.do(lambda e: e.memset(partner, 0.0))
            sx.wait(ql_fw)
            prog.add("sync", lambda e: e.sem_inc(sems["sSX"], 1))
            sx.n += 1
            m_xdone = sx.n

        dv.wait(sx, m_xdone)
        dv.wait(dv, dv.n)
        dv.do(lambda e: e.tensor_tensor(kvloc, kvloc, partner, ALU.add))
        m_sum = dv.n
        dv.wait(dv, m_sum)
        stats_bf = partner.bitcast(BF16)
        SB_COS = stats_bf[0:64, 0:1040]
        SB_SIN = stats_bf[0:64, 1040:2080]
        dv.do(lambda e: e.tensor_copy(SB_COS, kvloc[0:64, :]))
        dv.do(lambda e: e.tensor_copy(SB_SIN, kvloc[64:128, :]))
        m_stats_bf = dv.n

        ql_wq.wait(pe, m_kv_mms)
        ql_wq.do(lambda e: e.dma_start(
            out=W1[:, :].rearrange("p (c f) -> p c f", c=8),
            in_=wqkv[:, 0:1024].rearrange("(c p) f -> p c f", p=128)))
        ql_wo2.wait(pe, m_kv_mms)
        ql_wo2.do(lambda e: e.dma_start(
            out=W2[:, :].rearrange("p (c f) -> p c f", c=8),
            in_=wout[:, :].rearrange("(c p) f -> p c f", p=128)))
        WQ = W1[:, :].rearrange("p (c f) -> p c f", c=8)
        WO = W2[:, :].rearrange("p (c f) -> p c f", c=8)

        QR = [W4[0:64, 0:256], W4[0:64, 256:512]]
        QC = [W4[0:64, 512:768], W4[0:64, 768:1024]]
        QS = [W4[0:64, 1024:1280], W4[0:64, 1280:1536]]
        CS = [W4[:, 1536:2048].bitcast(F32)[0:64, 0:256],
              W4[:, 2048:2560].bitcast(F32)[0:64, 0:256]]
        CSROW = [W4[:, 1536:2048].bitcast(F32)[0:1, 0:256],
                 W4[:, 2048:2560].bitcast(F32)[0:1, 0:256]]
        RNR = [W4[0:1, 2560:2816], W4[0:1, 2816:3072]]

        pe.wait(ql_wq)
        pe.wait(dv, m_stats_bf)
        ac.wait(dv, m_stats_bf)
        NIT = n_ag * H
        M = [dict() for _ in range(NIT)]

        def hg(it):
            return it // H, it % H

        def emit_q(it):
            g, h = hg(it)
            pp = it % 2
            qb = pb[4 + pp]
            bq_col = bq_t[64 * (h % 2):64 * (h % 2) + 64, h // 2:h // 2 + 1]
            if it >= 2:
                pe.wait(ac, M[it - 2]["qr"])
            for k in range(KCH):
                pe.do(lambda e, k=k, g=g, h=h, qb=qb: e.matmul(
                    qb[0:64, 0:256], WQ[:, k, 64 * h:64 * h + 64],
                    AX[:, k, g * 256:(g + 1) * 256],
                    start=(k == 0), stop=(k == KCH - 1)),
                    inc=(k == KCH - 1))
            M[it]["q"] = pe.n
            ac.wait(pe, M[it]["q"])
            if it >= 2:
                ac.wait(dv, M[it - 2]["qcs"])
            ac.do(lambda e, qb=qb, pp=pp, bq_col=bq_col: e.activation(
                QR[pp], qb[0:64, 0:256], AF.Relu, bias=bq_col))
            M[it]["qr"] = ac.n
            dv.wait(ac, M[it]["qr"])
            if it >= 2:
                dv.wait(pe, M[it - 2]["ctx"])
            dv.do(lambda e, pp=pp, g=g: e.tensor_tensor(
                QC[pp], QR[pp], COSB[0:64, g * 256:(g + 1) * 256], ALU.mult))
            dv.do(lambda e, pp=pp, g=g: e.tensor_tensor(
                QS[pp], QR[pp], SINB[0:64, g * 256:(g + 1) * 256], ALU.mult))
            M[it]["qcs"] = dv.n

        def emit_ctx(it):
            g, h = hg(it)
            pp = it % 2
            cb = pb[pp]
            pe.wait(dv, M[it]["qcs"])
            if it >= 2:
                pe.wait(ac, M[it - 2]["cs"])
            pe.do(lambda e, h=h, pp=pp, cb=cb: e.matmul(
                cb[0:65, 0:256], SB_COS[:, 65 * h:65 * h + 65], QC[pp],
                start=True, stop=False), inc=False)
            pe.do(lambda e, h=h, pp=pp, cb=cb: e.matmul(
                cb[0:65, 0:256], SB_SIN[:, 65 * h:65 * h + 65], QS[pp],
                start=False, stop=True))
            M[it]["ctx"] = pe.n
            ac.wait(pe, M[it]["ctx"])
            if it >= 2:
                ac.wait(dv, M[it - 2]["tt"])
            ac.do(lambda e, cb=cb, pp=pp: e.activation(
                CSROW[pp], cb[64:65, 0:256], AF.Ln, bias=consts[64:65, 2:3]))
            m_l = ac.n
            ac.wait(ac, m_l)
            if it >= 2:
                ac.wait(pe, M[it - 2]["nb"])
            ac.do(lambda e, pp=pp: e.activation(
                RNR[pp][0:1, 0:256], CSROW[pp], AF.Exp,
                scale=consts[0:1, 3:4]))
            M[it]["rn"] = ac.n
            ac.wait(ac, M[it]["rn"])
            ac.do(lambda e, cb=cb, pp=pp: e.copy(CS[pp], cb[0:64, 0:256]))
            M[it]["cs"] = ac.n

        def emit_nb(it):
            g, h = hg(it)
            pp = it % 2
            nb = pb[2 + pp]
            pe.wait(ac, M[it]["rn"])
            if it >= 2:
                pe.wait(dv, M[it - 2]["tt"])
            pe.do(lambda e, pp=pp, nb=nb: e.matmul(
                nb[0:64, 0:256], ones[0:1, 0:64], RNR[pp][0:1, 0:256],
                start=True, stop=True))
            M[it]["nb"] = pe.n
            dv.wait(pe, M[it]["nb"])
            dv.wait(ac, M[it]["cs"])
            ro = 64 * (h % 2)
            dv.do(lambda e, h=h, g=g, pp=pp, nb=nb, ro=ro: e.tensor_tensor(
                AQ[ro:ro + 64, h // 2, g * 256:(g + 1) * 256],
                CS[pp], nb[0:64, 0:256], ALU.mult))
            M[it]["tt"] = dv.n

        for it in range(NIT + 2):
            if it < NIT:
                emit_q(it)
            if 0 <= it - 1 < NIT:
                emit_ctx(it - 1)
            if 0 <= it - 2 < NIT:
                emit_nb(it - 2)
        m_ctx_done = dv.n

        gT = AK[:, 0:64].bitcast(F32)[:, 0:16]
        top8 = AK[:, 64:128].bitcast(F32)[:, 0:8]
        msk = AK[:, 128:192].bitcast(F32)[:, 0:16]
        gsum = AK[:, 192:200].bitcast(F32)[:, 0:1]
        combine_F = partner.bitcast(BF16)[0:16, 0:2048]
        gate_state = {"m_sig": 0}

        ln2_apply = {}

        def emit_ln2(g):
            emit_ln(lambda k, g=g: AR[:, k, g * 256:(g + 1) * 256],
                    cast_needed=True,
                    apply=apply_to_bf16(
                        lambda k, g=g: AX[:, k, g * 256:(g + 1) * 256],
                        lambda k, g=g: AR[:, k, g * 256:(g + 1) * 256]),
                    W=256, xa=xbt[:, 0:512], xs=xbt[:, 512:1024])
            ln2_apply[g] = dv.n
            return dv.n

        def emit_gate(c):
            if gate_state["m_sig"]:
                pe.wait(ac, gate_state["m_sig"])
            pe.wait(dv, ln2_apply[c // 2])
            pe.wait(dv, emit_ln.last_lnb)
            for k in range(KCH):
                pe.do(lambda e, k=k, c=c: e.matmul(
                    pb[6][0:128, 0:16], AX[:, k, c * 128:(c + 1) * 128],
                    gwTb[:, 16 * k:16 * k + 16],
                    start=(k == 0), stop=(k == KCH - 1)), inc=(k == KCH - 1))
            m_g = pe.n
            ac.wait(pe, m_g)
            ac.wait(dv, dv.n)
            ac.do(lambda e: e.activation(gT, pb[6][:, 0:16], AF.Sigmoid))
            m_sig = ac.n
            gate_state["m_sig"] = m_sig
            dv.wait(ac, m_sig)
            dv.do(lambda e: e.max(out=top8, in_=gT))
            dv.wait(dv, dv.n)
            dv.do(lambda e: e.tensor_scalar(msk, gT, top8[:, 3:4], None, ALU.is_ge))
            dv.wait(dv, dv.n)
            dv.do(lambda e: e.tensor_tensor(msk, msk, gT, ALU.mult))
            dv.wait(dv, dv.n)
            dv.do(lambda e: e.reduce_sum(gsum, msk, axis=X))
            dv.wait(dv, dv.n)
            dv.do(lambda e: e.reciprocal(gsum, gsum))
            dv.wait(dv, dv.n)
            dv.do(lambda e, c=c: e.tensor_scalar(
                combT[:, c * 16:(c + 1) * 16], msk, gsum[:, 0:1], None, ALU.mult))

        phase8_pieces = []
        for g in range(0, n_ag // 2):
            phase8_pieces.append((lambda g=g: emit_ln2(g), g))
            phase8_pieces.append((lambda g=g: emit_gate(2 * g), g))
            phase8_pieces.append((lambda g=g: emit_gate(2 * g + 1), g))

        YB2 = W1[:, 0:4096].rearrange("p (c t) -> p c t", c=8)
        pe.wait(ql_wo2)
        pe.wait(dv, m_ctx_done)
        op_evac = {}
        ln3_ac = {}

        def emit_outproj(g):
            yo = (g % 2) * 256
            if (g - 2) in ln3_ac:
                dv.wait(ac, ln3_ac[g - 2])
            for m in range(KCH):
                bank = pb[m % 2]
                prev = op_evac.get((g, m - 2)) or op_evac.get((g - 1, m + 6))
                if prev is not None:
                    pe.wait(dv, prev)
                for k in range(KCH):
                    pe.do(lambda e, k=k, m=m, g=g, bank=bank: e.matmul(
                        bank[:, 0:256], WO[:, k, 128 * m:128 * m + 128],
                        AQ[:, k, g * 256:(g + 1) * 256],
                        start=(k == 0), stop=(k == KCH - 1)),
                        inc=(k == KCH - 1))
                m_mm = pe.n
                dv.wait(pe, m_mm)
                dv.do(lambda e, m=m, bank=bank, yo=yo: e.tensor_scalar(
                    YB2[:, m, yo:yo + 256], bank[:, 0:256], bo_t[:, m:m + 1],
                    None, ALU.add))
                op_evac[(g, m)] = dv.n
            return dv.n

        def emit_ln3(g):
            yo = (g % 2) * 256
            m_y = op_evac[(g, KCH - 1)]
            emit_ln(lambda k, yo=yo: YB2[:, k, yo:yo + 256], cast_needed=False,
                    apply=apply_residual(
                        lambda k, yo=yo: YB2[:, k, yo:yo + 256],
                        lambda k, g=g: AR[:, k, g * 256:(g + 1) * 256]),
                    extra_waits=lambda m_y=m_y: (pe.wait(dv, m_y),
                                                 ac.wait(dv, m_y)), W=256)
            ln3_ac[g] = ac.n

        for g in range(n_ag + 1):
            if g < n_ag:
                emit_outproj(g)
            if g >= 1:
                emit_ln3(g - 1)
                while phase8_pieces and phase8_pieces[0][1] <= g - 1:
                    phase8_pieces.pop(0)[0]()
        while phase8_pieces:
            phase8_pieces.pop(0)[0]()
        m_x1 = dv.n

        m_c_half = {}
        m_c_half[0] = dv.n
        pieces = []
        g0h1, g1h1 = n_ag // 2, n_ag
        for g in range(g0h1, g1h1):
            pieces.append(lambda g=g: emit_ln2(g))
            pieces.append(lambda g=g: emit_gate(2 * g))
            pieces.append(lambda g=g: emit_gate(2 * g + 1))
        m_xn2 = dv.n

        ACCF = AQ[:, :, :].bitcast(F32)
        HEV = AC[:, 0:4 * FH].rearrange("p (c t) -> p c t", c=4)
        WVA = [W1[:, :].rearrange("p (c f) -> p c f", c=16),
               W2[:, :].rearrange("p (c f) -> p c f", c=16)]
        WOB = [W3[:, 0:4096].rearrange("p (c f) -> p c f", c=4),
               W4[:, 0:4096].rearrange("p (c f) -> p c f", c=4)]
        SIL = AK[:, 0:1024].bitcast(F32)[:, 0:512]
        TMP = AK[:, 1024:2048].bitcast(F32)[:, 0:512]

        def win_ap(e_):
            if e_ < 16:
                return ewin[e_].rearrange("(c p) h -> p c h", p=128)
            return wsin[:, (e_ - 16) * 512:(e_ - 15) * 512]\
                .rearrange("(c p) h -> p c h", p=128)

        def vin_ap(e_):
            if e_ < 16:
                return evin[e_].rearrange("(c p) h -> p c h", p=128)
            return wvin[:, (e_ - 16) * 512:(e_ - 15) * 512]\
                .rearrange("(c p) h -> p c h", p=128)

        def wout_ap(e_):
            if e_ < 16:
                return ewout[e_].rearrange("(c p) d -> p c d", p=128)
            return wsout[(e_ - 16) * 512:(e_ - 15) * 512, :]\
                .rearrange("(c p) d -> p c d", p=128)

        wv_marks, wo_marks, a_done, b_done, badd_last = {}, {}, {}, {}, {}

        def load_wv(idx, e_):
            bi = idx % 2
            if (idx - 2) in a_done:
                ql_wv[bi].wait(pe, a_done[idx - 2])
            ql_wv[bi].do(lambda e, e_=e_, bi=bi: e.dma_start(
                out=WVA[bi][:, 0:8, :], in_=win_ap(e_)))
            ql_wv[bi].do(lambda e, e_=e_, bi=bi: e.dma_start(
                out=WVA[bi][:, 8:16, :], in_=vin_ap(e_)))
            wv_marks[idx] = (ql_wv[bi], ql_wv[bi].n)

        def load_wo(idx, e_):
            bi = idx % 2
            if (idx - 2) in b_done:
                ql_wb[bi].wait(pe, b_done[idx - 2])
            ql_wb[bi].do(lambda e, e_=e_, bi=bi: e.dma_start(
                out=WOB[bi][:, :, :], in_=wout_ap(e_)))
            wo_marks[idx] = (ql_wb[bi], ql_wb[bi].n)

        for qq in (ql_wv[0], ql_wv[1], ql_wb[0], ql_wb[1]):
            qq.wait(dv, m_x1)
            qq.wait(dv, m_ctx_done)
            qq.wait(ac, ac.n)

        def emit_ln4_store(half):
            m_acc = dv.n
            WL = 256 if half == 0 else 512
            xa = xbt[:, 0:512] if WL == 256 else AC[:, 0:1024]
            xs = xbt[:, 512:1024] if WL == 256 else AC[:, 1024:2048]
            for gg in range(FH // WL):
                goff = half * FH + gg * WL
                loff = gg * WL
                emit_ln(lambda k, loff=loff, WL=WL: ACCF[:, k, loff:loff + WL],
                        cast_needed=True,
                        apply=apply_residual(
                            lambda k, loff=loff, WL=WL: ACCF[:, k, loff:loff + WL],
                            lambda k, goff=goff, WL=WL: AR[:, k, goff:goff + WL],
                            W=WL),
                        extra_waits=lambda m_acc=m_acc: pe.wait(dv, m_acc),
                        W=WL, xa=xa, xs=xs)
                m_g_done = dv.n
                h2.wait(dv, m_g_done)
                h2.do(lambda e, half=half, goff=goff, WL=WL: e.dma_start(
                    out=outT[:, goff:goff + WL]
                    .rearrange("(c p) t -> p c t", p=128),
                    in_=AR[:, :, goff:goff + WL]))

        seq = [(half, e_) for half in range(2) for e_ in range(NEXP)]
        load_wv(0, seq[0][1])
        load_wo(0, seq[0][1])
        load_wv(1, seq[1][1])
        load_wo(1, seq[1][1])

        for idx, (half, e_) in enumerate(seq):
            bi = idx % 2
            gated = e_ < 16
            hbase = half * FH
            pe.wait(*wv_marks[idx])
            if idx == 0:
                pe.wait(dv, m_c_half[0])
            if idx == NEXP:
                pe.wait(dv, m_c_half[1])
            chain_marks = {}
            for sub in range(nsub):
                t0 = hbase + sub * FSUB
                ncw = FSUB // 128
                if gated:
                    cbb = pb[4 + sub % 2]
                    if (idx - 1) in b_done:
                        pe.wait(dv, badd_last[idx - 1])
                    else:
                        pe.wait(dv, dv.n)
                    for ci in range(ncw):
                        cgl = (t0 // 128) + ci
                        pe.do(lambda e, e_=e_, cgl=cgl, ci=ci, cbb=cbb: e.matmul(
                            cbb[0:1, ci * 128:ci * 128 + 128],
                            combT[:, cgl * 16 + e_:cgl * 16 + e_ + 1],
                            identb[:, :], start=True, stop=True))
                    m_sel = pe.n
                    csl = crow[0:1, 0:FSUB]
                    dv.wait(pe, m_sel)
                    dv.do(lambda e, csl=csl, cbb=cbb, w=FSUB: e.tensor_copy(
                        csl, cbb[0:1, 0:w]))
                    m_crow = dv.n
                    pe.wait(dv, m_crow)
                    pe.do(lambda e, csl=csl, cbb=cbb, w=FSUB: e.matmul(
                        cbb[:, 0:w], ones[0:1, :], csl, start=True, stop=True))
                for hc in range(HIDC):
                    hb = pb[hc % 2]
                    vb = pb[2 + hc % 2]
                    prevc = chain_marks.get((sub, hc - 2))
                    if prevc is not None:
                        pe.wait(dv, prevc[0])
                        pe.wait(ac, prevc[1])
                    for k in range(KCH):
                        pe.do(lambda e, k=k, hc=hc, t0=t0, bi=bi, hb=hb, w=FSUB:
                              e.matmul(hb[:, 0:w],
                                       WVA[bi][:, k, 128 * hc:128 * hc + 128],
                                       AX[:, k, t0:t0 + w],
                                       start=(k == 0), stop=(k == KCH - 1)),
                              inc=(k == KCH - 1))
                    m_hin = pe.n
                    for k in range(KCH):
                        pe.do(lambda e, k=k, hc=hc, t0=t0, bi=bi, vb=vb, w=FSUB:
                              e.matmul(vb[:, 0:w],
                                       WVA[bi][:, 8 + k, 128 * hc:128 * hc + 128],
                                       AX[:, k, t0:t0 + w],
                                       start=(k == 0), stop=(k == KCH - 1)),
                              inc=(k == KCH - 1))
                    m_vin = pe.n
                    ac.wait(pe, m_hin)
                    prevt = chain_marks.get((sub, hc - 1))
                    if prevt is not None:
                        ac.wait(dv, prevt[0])
                    ac.do(lambda e, hb=hb, w=FSUB: e.activation(
                        SIL[:, 0:w], hb[:, 0:w], AF.Silu))
                    m_sil = ac.n
                    dv.wait(ac, m_sil)
                    dv.wait(pe, m_vin)
                    ht = HEV[:, hc, sub * FSUB:sub * FSUB + FSUB]
                    if gated:
                        dv.do(lambda e, vb=vb, w=FSUB: e.tensor_tensor(
                            TMP[:, 0:w], SIL[:, 0:w], vb[:, 0:w], ALU.mult))
                        mt = dv.n
                        dv.wait(dv, mt)
                        cbb = pb[4 + sub % 2]
                        dv.do(lambda e, ht=ht, cbb=cbb, w=FSUB: e.tensor_tensor(
                            ht, TMP[:, 0:w], cbb[:, 0:w], ALU.mult))
                    else:
                        dv.do(lambda e, ht=ht, vb=vb, w=FSUB: e.tensor_tensor(
                            ht, SIL[:, 0:w], vb[:, 0:w], ALU.mult))
                    chain_marks[(sub, hc)] = (dv.n, ac.n)
            a_done[idx] = pe.n
            m_h = dv.n
            if half == 0 and pieces:
                pieces.pop(0)()
                if e_ == 0 and pieces:
                    pieces.pop(0)()
                if not pieces:
                    m_c_half[1] = dv.n
            if half == 1 and e_ == 0:
                emit_ln4_store(0)
            pe.wait(*wo_marks[idx])
            pe.wait(dv, m_h)
            badd = {}
            if e_ == 0:
                pe.wait(ac, ac.n)
            for sub in range(nsub):
                for m in range(KCH):
                    mi = m + sub * KCH
                    bb = pb[4 + mi % 2]
                    prevb = badd.get(mi - 2)
                    if prevb is not None:
                        pe.wait(dv, prevb)
                    for k in range(HIDC):
                        pe.do(lambda e, k=k, m=m, sub=sub, bi=bi, bb=bb, w=FSUB:
                              e.matmul(bb[:, 0:w],
                                       WOB[bi][:, k, 128 * m:128 * m + 128],
                                       HEV[:, k, sub * FSUB:sub * FSUB + w],
                                       start=(k == 0), stop=(k == HIDC - 1)),
                              inc=(k == HIDC - 1))
                    m_b = pe.n
                    dv.wait(pe, m_b)
                    accsl = ACCF[:, m, sub * FSUB:(sub + 1) * FSUB]
                    if e_ == 0:
                        dv.do(lambda e, bb=bb, accsl=accsl, w=FSUB:
                              e.tensor_copy(accsl, bb[:, 0:w]))
                    else:
                        dv.do(lambda e, bb=bb, accsl=accsl, w=FSUB:
                              e.tensor_tensor(accsl, accsl, bb[:, 0:w], ALU.add))
                    badd[mi] = dv.n
            badd_last[idx] = dv.n
            b_done[idx] = pe.n
            if idx + 2 < len(seq):
                load_wv(idx + 2, seq[idx + 2][1])
                load_wo(idx + 2, seq[idx + 2][1])

        emit_ln4_store(1)
        m_store = h2.n
        prog.add("sync", lambda e: e.wait_ge(sems["sH2"], m_store))

        with nc.Block() as blk:
            @blk.sync
            def _(e):
                for fn in prog.t["sync"]:
                    fn(e)

            @blk.gpsimd
            def _(e):
                for fn in prog.t["gpsimd"]:
                    fn(e)

            @blk.tensor
            def _(e):
                for fn in prog.t["tensor"]:
                    fn(e)

            @blk.scalar
            def _(e):
                for fn in prog.t["scalar"]:
                    fn(e)

            @blk.vector
            def _(e):
                for fn in prog.t["vector"]:
                    fn(e)

    return nc


def make_in_maps(cfg, A):
    NO = cfg["NO"]
    NF = 3 * NO
    S, FS = NO * 128, NF * 128
    L = 8 * S
    x = np.ascontiguousarray(A["x"].reshape(L, DM).astype(np.float32))
    xT = np.ascontiguousarray(x.T)
    pos = np.arange(L, dtype=np.float64)
    cw = np.cos(np.pi / 2 * pos / L).astype(np.float32)
    sw = np.sin(np.pi / 2 * pos / L).astype(np.float32)

    b_qkv = A["b_qkv"].astype(np.float32)
    com = dict(
        wqkv=np.ascontiguousarray(A["W_qkv"].astype(np.float32)),
        wout=np.ascontiguousarray(A["W_out"].astype(np.float32)),
        gwt=np.ascontiguousarray(
            A["gate_w"].astype(np.float32).T.reshape(8, 128, 16)
            .transpose(1, 0, 2).reshape(128, 128)),
        wsin=np.ascontiguousarray(A["ws_in"].astype(np.float32)),
        wvin=np.ascontiguousarray(A["vs_in"].astype(np.float32)),
        wsout=np.ascontiguousarray(A["ws_out"].astype(np.float32)),
        ewin=np.ascontiguousarray(A["e_win"].astype(np.float32)),
        evin=np.ascontiguousarray(A["e_vin"].astype(np.float32)),
        ewout=np.ascontiguousarray(A["e_wout"].astype(np.float32)),
        bqt=np.ascontiguousarray(b_qkv[0:1024].reshape(8, 128).T),
        bkv=np.ascontiguousarray(b_qkv[1024:3072].reshape(1, 2048)),
        bot=np.ascontiguousarray(A["b_out"].astype(np.float32).reshape(8, 128).T),
        identin=np.eye(128, dtype=np.float32),
    )
    maps = []
    for c in range(8):
        own = np.arange(c * S, (c + 1) * S)
        pair = c // 2
        fidx = np.concatenate([np.arange(0, pair * 2 * S),
                               np.arange((pair + 1) * 2 * S, L)])
        fidx = fidx[:FS] if c % 2 == 0 else fidx[FS:]
        chunk_pos = np.concatenate([own, fidx])
        m = dict(com)
        m["xTo"] = np.ascontiguousarray(xT[:, own])
        m["xTf"] = np.ascontiguousarray(xT[:, fidx])
        m["cosbo"] = np.ascontiguousarray(np.broadcast_to(cw[own], (128, S)))
        m["sinbo"] = np.ascontiguousarray(np.broadcast_to(sw[own], (128, S)))
        m["cost"] = np.ascontiguousarray(cw[chunk_pos].reshape(-1, 128).T)
        m["sint"] = np.ascontiguousarray(sw[chunk_pos].reshape(-1, 128).T)
        maps.append(m)
    return maps


def gather_out(cfg, results):
    NO = cfg["NO"]
    S = NO * 128
    L = 8 * S
    outs = [np.asarray(r["outT"], dtype=np.float32) for r in results]
    full = np.concatenate(outs, axis=1)
    return np.ascontiguousarray(full.T).reshape(1, L, DM)


def kernel(L0_attn_x, L0_attn_W_qkv, L0_attn_b_qkv, L0_attn_W_out, L0_attn_b_out,
           L0_ffn_gate_w, L0_ffn_ws_in, L0_ffn_vs_in, L0_ffn_ws_out,
           L0_ffn_e_win, L0_ffn_e_vin, L0_ffn_e_wout):
    cfg = REAL
    A = dict(x=np.asarray(L0_attn_x), W_qkv=np.asarray(L0_attn_W_qkv),
             b_qkv=np.asarray(L0_attn_b_qkv), W_out=np.asarray(L0_attn_W_out),
             b_out=np.asarray(L0_attn_b_out), gate_w=np.asarray(L0_ffn_gate_w),
             ws_in=np.asarray(L0_ffn_ws_in), vs_in=np.asarray(L0_ffn_vs_in),
             ws_out=np.asarray(L0_ffn_ws_out), e_win=np.asarray(L0_ffn_e_win),
             e_vin=np.asarray(L0_ffn_e_vin), e_wout=np.asarray(L0_ffn_e_wout))
    nc = build(cfg)
    in_maps = make_in_maps(cfg, A)
    res = run_bass_kernel_spmd(nc, in_maps, core_ids=list(range(8)))
    return gather_out(cfg, res.results)
